# version 25
# baseline (speedup 1.0000x reference)
import numpy as np
from contextlib import ExitStack

import concourse.bass as bass
import concourse.mybir as mybir
from concourse.bass_utils import run_bass_kernel_spmd

F32 = mybir.dt.float32
BF16 = mybir.dt.bfloat16
I32 = mybir.dt.int32
AF = mybir.ActivationFunctionType
ALU = mybir.AluOpType
AX = mybir.AxisListType

NC = 8


def full_cfg():
    return dict(B=2, S=4096, E=2048, H=16, F=8192, R=16)


def derive(cfg):
    c = dict(cfg)
    B, S, E, H, F, R = c["B"], c["S"], c["E"], c["H"], c["F"], c["R"]
    assert E == H * 128 and F % 128 == 0 and S % NC == 0
    c["TPC"] = B * S // NC
    c["QL"] = S // NC
    c["TTOK"] = min(512, c["TPC"])
    assert c["TPC"] % c["TTOK"] == 0
    c["NTT"] = c["TPC"] // c["TTOK"]
    c["NDC"] = E // 128
    c["NHC"] = F // 128
    c["NJC"] = 3 * E // 128
    c["ESL"] = E // NC
    c["FSL"] = F // NC
    assert c["ESL"] % 128 == 0 and c["FSL"] % 128 == 0
    assert c["QL"] % 128 == 0
    c["NQT"] = c["QL"] // 128
    c["QTW"] = 128
    assert S % 1024 == 0 or S == 512
    c["KR"] = [1024] * (S // 1024) if S >= 1024 else [512]
    c["NKC"] = S // 128
    assert c["TPC"] % 512 == 0 or c["TPC"] in (128, 256)
    return c


class Sem:
    def __init__(self, stack, nc, name):
        self.h = stack.enter_context(nc.semaphore(name))
        self.n = 0

    def tok(self, k=1):
        self.n += k
        return self.n


def build(cfg):
    c = derive(cfg)
    B, S, E, H, F, R = c["B"], c["S"], c["E"], c["H"], c["F"], c["R"]
    TPC, QL, TTOK, NTT = c["TPC"], c["QL"], c["TTOK"], c["NTT"]
    NDC, NHC, NJC, ESL, FSL = c["NDC"], c["NHC"], c["NJC"], c["ESL"], c["FSL"]
    NQT, QTW, KR, NKC = c["NQT"], c["QTW"], c["KR"], c["NKC"]
    NH512 = TPC // 512 if TPC >= 512 else 1
    CW = min(512, TPC)
    NU = B * H

    nc = bass.Bass("TRN2", target_bir_lowering=False, debug=False)

    xs = nc.dram_tensor("xs", [E, TPC], F32, kind="ExternalInput").ap()
    wupq_s = nc.dram_tensor("wupq_s", [F, ESL], I32, kind="ExternalInput").ap()
    wups_s = nc.dram_tensor("wups_s", [F, ESL // 64], F32, kind="ExternalInput").ap()
    wdnq_s = nc.dram_tensor("wdnq_s", [E, FSL], I32, kind="ExternalInput").ap()
    wdns_s = nc.dram_tensor("wdns_s", [E, FSL // 64], F32, kind="ExternalInput").ap()
    wqkv_s = nc.dram_tensor("wqkv_s", [3 * E, ESL], F32, kind="ExternalInput").ap()
    wproj_s = nc.dram_tensor("wproj_s", [E, ESL], F32, kind="ExternalInput").ap()
    bqkv = nc.dram_tensor("bqkv", [NJC, 128], F32, kind="ExternalInput").ap()
    bproj = nc.dram_tensor("bproj", [NDC, 128], F32, kind="ExternalInput").ap()
    bup = nc.dram_tensor("bup", [NHC, 128], F32, kind="ExternalInput").ap()
    bdn = nc.dram_tensor("bdn", [NDC, 128], F32, kind="ExternalInput").ap()
    aup = nc.dram_tensor("aup", [E, R], F32, kind="ExternalInput").ap()
    bupl = nc.dram_tensor("bupl", [R, F], F32, kind="ExternalInput").ap()
    adn = nc.dram_tensor("adn", [F, R], F32, kind="ExternalInput").ap()
    bdnl = nc.dram_tensor("bdnl", [R, E], F32, kind="ExternalInput").ap()
    y = nc.dram_tensor("y", [E, TPC], F32, kind="ExternalOutput").ap()

    wupnat = nc.dram_tensor("wupnat", [F, ESL], BF16, kind="Internal").ap()
    wdnnat = nc.dram_tensor("wdnnat", [E, FSL], BF16, kind="Internal").ap()
    wqknat = nc.dram_tensor("wqknat", [3 * E, ESL], BF16, kind="Internal").ap()
    wprnat = nc.dram_tensor("wprnat", [E, ESL], BF16, kind="Internal").ap()

    up_sz, dn_sz, qk_sz, pr_sz = ESL * F, FSL * E, ESL * 3 * E, ESL * E
    wt_elems = up_sz + dn_sz + qk_sz + pr_sz
    wT_loc = nc.dram_tensor("wT_loc", [wt_elems // 1024, 1024], BF16, kind="Internal").ap()
    NUS = 2
    wT_gus = [nc.dram_tensor(f"wT_gu{i}", [NC * (up_sz // 1024 // NUS), 1024], BF16,
                             kind="Internal", addr_space="Shared").ap() for i in range(NUS)]
    wT_gd = nc.dram_tensor("wT_gd", [NC * (dn_sz // 1024), 1024], BF16, kind="Internal", addr_space="Shared").ap()
    wT_gr = nc.dram_tensor("wT_gr", [NC * ((qk_sz + pr_sz) // 1024), 1024], BF16, kind="Internal", addr_space="Shared").ap()
    wTgUs = [g.rearrange("(r x) cc -> r (x cc)", r=NC) for g in wT_gus]
    wTgD = wT_gd.rearrange("(r x) cc -> r (x cc)", r=NC)
    wTgR = wT_gr.rearrange("(r x) cc -> r (x cc)", r=NC)

    HG0 = max(2, H // 4)
    HG1 = H - HG0
    def kv_group(h):
        return 0 if h < HG0 else 1
    def kv_kbase(h):
        g = kv_group(h)
        hl = h - (0 if g == 0 else HG0)
        ghg = HG0 if g == 0 else HG1
        base = 0 if g == 0 else 2 * HG0 * 128 * TPC
        return base + hl * 128 * TPC, g
    def kv_vbase(h):
        g = kv_group(h)
        hl = h - (0 if g == 0 else HG0)
        ghg = HG0 if g == 0 else HG1
        base = 0 if g == 0 else 2 * HG0 * 128 * TPC
        return base + ghg * 128 * TPC + hl * TPC * 128, g
    GH0 = 2 * HG0 * 128 * TPC
    GH1 = 2 * HG1 * 128 * TPC
    kv_elems = GH0 + GH1
    kv_loc = nc.dram_tensor("kv_loc", [kv_elems // 1024, 1024], BF16, kind="Internal").ap()
    kv_g0 = nc.dram_tensor("kv_g0", [NC * (GH0 // 1024), 1024], BF16, kind="Internal", addr_space="Shared").ap()
    kv_g1 = nc.dram_tensor("kv_g1", [NC * (GH1 // 1024), 1024], BF16, kind="Internal", addr_space="Shared").ap()
    q_loc = nc.dram_tensor("q_loc", [H, 128, TPC], BF16, kind="Internal").ap()

    def wview(gsrc, off, rows_per_rank, ncols):
        v3 = gsrc[:, off:off + rows_per_rank * ncols].rearrange("r (d cc) -> r d cc", cc=ncols)
        nchunk = rows_per_rank // 128
        def block(c0, w):
            return v3[:, :, c0:c0 + w].rearrange("r (dl p) f -> p r dl f", p=128), nchunk
        return block
    upTs = [wview(g, 0, ESL, F // NUS) for g in wTgUs]
    dnT = wview(wTgD, 0, FSL, E)
    qkT = wview(wTgR, 0, ESL, 3 * E)
    prT = wview(wTgR, qk_sz, ESL, E)
    kvf = kv_loc.rearrange("a b -> (a b)")
    kvgs = [kv_g0.rearrange("(r x) cc -> r (x cc)", r=NC),
            kv_g1.rearrange("(r x) cc -> r (x cc)", r=NC)]
    wTf = wT_loc.rearrange("a b -> (a b)")

    deq_w = 4 * max(ESL, FSL)
    NPQ = 4
    at_ln = NPQ * deq_w
    tr_w = max(F, 3 * E, 2048)
    tr0 = at_ln + 5 * TPC
    scl0 = tr0 + tr_w
    CB = max(NHC * TTOK, S + 14 * QL, scl0 + 4 * 64)
    FB = max(2 * NHC * 128, NU * QL, 2 * F)
    GB = 2 * NDC * 128
    STG = max(2 * TPC, 4 * TTOK)

    with ExitStack() as st_:
        en = st_.enter_context
        x_sb = en(nc.sbuf_tensor("x_sb", [128, NDC * TPC], F32))
        xn_sb = en(nc.sbuf_tensor("xn_sb", [128, max(NDC * TPC, 2 * S + 2 * NKC * 128)], BF16))
        cbig = en(nc.sbuf_tensor("cbig", [128, CB], BF16))
        fbig = en(nc.sbuf_tensor("fbig", [128, FB], BF16))
        gbuf = en(nc.sbuf_tensor("gbuf", [128, GB], BF16))
        stage = en(nc.sbuf_tensor("stage", [128, STG], BF16))
        ones_bf = en(nc.sbuf_tensor("ones_bf", [1, 128], BF16))
        ones_col = en(nc.sbuf_tensor("ones_col", [128, 1], BF16))
        idbf = en(nc.sbuf_tensor("idbf", [128, 128], BF16))
        aup_sb = en(nc.sbuf_tensor("aup_sb", [128, NDC * R], BF16))
        adn_sb = en(nc.sbuf_tensor("adn_sb", [128, NHC * R], BF16))
        bupc = en(nc.sbuf_tensor("bupc", [128, NHC], BF16))
        bdnc = en(nc.sbuf_tensor("bdnc", [128, NDC], BF16))
        bqkc = en(nc.sbuf_tensor("bqkc", [128, NJC], BF16))
        bprc = en(nc.sbuf_tensor("bprc", [128, NDC], BF16))
        braw = en(nc.sbuf_tensor("braw", [max(NHC, NJC, NDC), 128], BF16))
        sm = en(nc.sbuf_tensor("sm", [128, 64], F32))
        PS = en(nc.psum_tensor("PS", [128, 3072], F32))
        PSB = en(nc.psum_tensor("PSB", [128, 1024], BF16))
        PSC = en(nc.psum_tensor("PSC", [128, 512], F32))

        ln_xsq = at_ln
        ln_mbc = at_ln + TPC
        ln_rbc = at_ln + 2 * TPC
        ln_rows = at_ln + 3 * TPC

        S_ = {}
        for name in ["dXL", "dW", "dG", "dLOR", "dNAT", "ns0", "ns1", "ns2", "ns3",
                     "wtA", "wtB", "dST0", "dST1",
                     "wsA", "wsB", "pq0", "pq1", "pq2", "pq3", "trA", "trB",
                     "kvA", "kvB", "kvC", "vvA", "vvB", "qA", "qB", "qC", "b32s",
                     "eT", "eV", "eS", "eG", "ccs", "dOUT", "dINIT"]:
            S_[name] = Sem(st_, nc, name)
        eT, eV, eS, eG = S_["eT"], S_["eV"], S_["eS"], S_["eG"]

        Q = {k: [] for k in "tvspg"}
        ST = {}

        def on(e, fn):
            Q[e].append(fn)

        def wait(e, sem, val):
            if val is None or val <= 0:
                return
            on(e, lambda eng, s=sem.h, v=val: eng.wait_ge(s, v))

        def dma(e, sem, out, in_):
            t = sem.tok(16)
            on(e, lambda eng, o=out, i=in_, s=sem.h: eng.dma_start(out=o, in_=i).then_inc(s, 16))
            return t

        def wblk(sem, dst2d, view, c0, nchunk):
            src4, nck = view(c0, 128)
            assert nck == nchunk
            d4 = dst2d.rearrange("p (r dl f) -> p r dl f", r=NC, dl=nchunk)
            t = None
            for dl in range(nchunk):
                t = dma("p", sem, d4[:, :, dl, :], src4[:, :, dl, :])
            return t

        def dma_tr(sem, out, in_):
            t = sem.tok(16)
            on("p", lambda eng, o=out, i=in_, s=sem.h: eng.dma_start_transpose(out=o, in_=i).then_inc(s, 16))
            return t

        def mm(out, lhsT, rhs, start, stop, inc=None):
            if inc is not None:
                t = inc.tok()
                on("t", lambda eng, o=out, l=lhsT, r=rhs, a=start, b=stop, s=inc.h:
                   eng.matmul(o, l, r, start=a, stop=b).then_inc(s, 1))
                return t
            on("t", lambda eng, o=out, l=lhsT, r=rhs, a=start, b=stop:
               eng.matmul(o, l, r, start=a, stop=b))
            return None

        def op(e, sem, meth, _lp=False, **kw):
            t = sem.tok()
            def f(eng, m=meth, kw=kw, s=sem.h, lp=_lp):
                if lp:
                    with nc.allow_low_precision(reason="deliberate bf16 intermediate"):
                        ins = getattr(eng, m)(**kw)
                else:
                    ins = getattr(eng, m)(**kw)
                ins.then_inc(s, 1)
            on(e, f)
            return t

        def tpose(out, in_, inc=None):
            np_ = in_.shape[0]
            ident = idbf[:] if np_ == 128 else idbf[0:np_, 0:np_]
            def f(eng, o=out, i=in_, d=ident, s=(inc.h if inc is not None else None)):
                with nc.allow_low_precision(reason="transpose is pure data movement"):
                    ins = eng.transpose(o, i, d)
                if s is not None:
                    ins.then_inc(s, 1)
            t = inc.tok() if inc is not None else None
            on("t", f)
            return t

        op("g", eG, "memset", ap=ones_bf[:], constant=1.0)
        op("g", eG, "memset", ap=ones_col[:], constant=1.0)
        t_idm = op("g", eG, "memset", ap=idbf[:], constant=0.0)
        wait("g", eG, t_idm)
        t_id = eG.tok()
        on("g", lambda eng: eng.affine_select(
            out=idbf[:], in_=idbf[:], compare_op=ALU.not_equal, fill=1.0,
            base=0, pattern=[[-1, 128]], channel_multiplier=1).then_inc(eG.h, 1))

        t_xl = None
        for dc in range(NDC):
            t_xl = dma("p", S_["dXL"], x_sb[:, dc * TPC:(dc + 1) * TPC],
                       xs[dc * 128:(dc + 1) * 128, :])

        t_aup = dma("g", S_["dLOR"], aup_sb[:].rearrange("p (d r) -> p d r", d=NDC),
                    aup.rearrange("(d p) r -> p d r", p=128))
        t_adn = t_aup
        for i in range(0, NHC, 16):
            nn = min(16, NHC - i)
            t_adn = dma("g", S_["dLOR"], adn_sb[:, i * R:(i + nn) * R].rearrange("p (d r) -> p d r", d=nn),
                        adn[i * 128:(i + nn) * 128, :].rearrange("(d p) r -> p d r", p=128))

        bias_specs = [(bqkv, NJC, bqkc, 1.0), (bup, NHC, bupc, 1.0),
                      (bdn, NDC, bdnc, 0.5), (bproj, NDC, bprc, 1.0)]
        t_bl = None
        for src, n, dst, scale in bias_specs:
            pass
        wait("t", S_["eG"], t_id)
        t_bias_done = None
        for bi, (src, n, dst, scale) in enumerate(bias_specs):
            if t_bias_done is not None:
                wait("g", eV, t_bias_done)
                wait("g", eT, ST[("btr", bi - 1)])
            t_bl = dma("g", S_["dINIT"], braw[0:n, :], src)
            wait("t", S_["dINIT"], t_bl)
            ST[("btr", bi)] = tpose(PSB[0:128, 0:n], braw[0:n, :], inc=eT)
            wait("v", eT, ST[("btr", bi)])
            if scale == 1.0:
                t_bias_done = op("v", eV, "tensor_copy", out=dst[:], in_=PSB[0:128, 0:n])
            else:
                t_bias_done = op("v", eV, "tensor_scalar", out=dst[:], in0=PSB[0:128, 0:n],
                                 scalar1=scale, scalar2=None, op0=ALU.mult)
        sqrt_hd = float(128.0 ** 0.5)
        wait("v", eV, t_bias_done)
        t_bias_done = op("v", eV, "tensor_scalar",
                         out=bqkc[:].rearrange("p (hh s) -> p hh s", s=3)[:, :, 0:1],
                         in0=bqkc[:].rearrange("p (hh s) -> p hh s", s=3)[:, :, 0:1],
                         scalar1=sqrt_hd, scalar2=None, op0=ALU.mult)

        pq_off = [i * deq_w for i in range(NPQ)]
        tr_off = [tr0]

        def prep_quant(qsrc, ssrc, nat, rows, cols):
            ngrp = rows // 512
            for g in range(ngrp):
                slot = g % NPQ
                sq = S_[f"pq{slot}"]
                ns = S_[f"ns{slot}"]
                prev = ST.get(("pq_deq", slot))
                if prev is not None:
                    wait("g", eV, prev)
                    wait("g", ns, ST[("pq_nat", slot)])
                tq = dma("g", sq, cbig[:, pq_off[slot]:pq_off[slot] + 4 * cols].rearrange(
                    "p (t cc) -> p t cc", t=4),
                    qsrc[g * 512:(g + 1) * 512, :].rearrange("(t p) cc -> p t cc", p=128))
                tq = dma("g", sq, cbig[:, scl0 + slot * 64:scl0 + slot * 64 + 4 * (cols // 64)].rearrange(
                    "p (t k) -> p t k", t=4),
                    ssrc[g * 512:(g + 1) * 512, :].rearrange("(t p) k -> p t k", p=128))
                wait("v", sq, tq)
                sexp = cbig[:, scl0 + slot * 64:scl0 + slot * 64 + 4 * (cols // 64)].rearrange(
                    "p (t k) -> p t k", t=4).unsqueeze(3).broadcast_to([128, 4, cols // 64, 64])
                tdq = op("v", eV, "scalar_tensor_tensor",
                         out=cbig[:, pq_off[slot]:pq_off[slot] + 4 * cols].rearrange(
                             "p (t k r) -> p t k r", t=4, k=cols // 64),
                         in0=cbig[:, pq_off[slot]:pq_off[slot] + 4 * cols].rearrange(
                             "p (t k r) -> p t k r", t=4, k=cols // 64),
                         scalar=-7.5, in1=sexp, op0=ALU.add, op1=ALU.mult)
                ST[("pq_deq", slot)] = tdq
                wait("g", eV, tdq)
                ST[("pq_nat", slot)] = dma("g", ns, nat[g * 512:(g + 1) * 512, :].rearrange(
                    "(t p) cc -> p t cc", p=128),
                    cbig[:, pq_off[slot]:pq_off[slot] + 4 * cols].rearrange("p (t cc) -> p t cc", t=4))
            return tuple(S_[f"ns{i}"].n for i in range(NPQ))

        t_nat_up = prep_quant(wupq_s, wups_s, wupnat, F, ESL)
        t_nat_dn = prep_quant(wdnq_s, wdns_s, wdnnat, E, FSL)
        ST["prep_dn_v"] = eV.n
        ST["prep_dn_ns"] = t_nat_dn
        t_cast_qk = dma("g", S_["dG"], wqknat, wqkv_s)
        t_cast_pr = dma("g", S_["dG"], wprnat, wproj_s)
        nw_up = [(S_[f"ns{i}"], t_nat_up[i]) for i in range(NPQ)]
        nw_dn = [(S_[f"ns{i}"], t_nat_dn[i]) for i in range(NPQ)]
        nw_cast = [(S_["dG"], t_cast_pr)]

        def prep_T(nat, rows, ncols, base, nat_waits):
            for i in range(ncols // 128):
                sq = S_["trA"]
                ws = S_["wtA"]
                for (nsem, ntok) in nat_waits:
                    wait("p", nsem, ntok)
                wait("p", ws, ws.n)
                ttr = dma_tr(sq, cbig[:, tr0:tr0 + rows],
                             nat[:, i * 128:(i + 1) * 128])
                wait("g", sq, ttr)
                dma("g", ws,
                    wTf[base + i * 128 * rows: base + (i + 1) * 128 * rows].rearrange(
                        "(p cc) -> p cc", p=128),
                    cbig[:, tr0:tr0 + rows])
            return None

        qF = F // NUS
        usz = up_sz // NUS
        for i in range(ESL // 128):
            sq = S_["trA"]
            ws = S_["wtA"]
            for (nsem, ntok) in nw_up:
                wait("p", nsem, ntok)
            wait("p", ws, ws.n)
            ttr = dma_tr(sq, cbig[:, tr0:tr0 + F], wupnat[:, i * 128:(i + 1) * 128])
            wait("g", sq, ttr)
            for r_ in range(NUS):
                dma("g", ws, wTf[r_ * usz + i * 128 * qF:
                                 r_ * usz + (i + 1) * 128 * qF].rearrange(
                    "(p cc) -> p cc", p=128),
                    cbig[:, tr0 + r_ * qF:tr0 + (r_ + 1) * qF])
        wait("g", S_["wtA"], S_["wtA"].n)
        t_ccwU = []
        for r_ in range(NUS):
            t_ccwU.append(S_["ccs"].tok())
            on("g", lambda eng, r2=r_: eng.collective_compute(
                "AllGather", ALU.bypass,
                ins=[wT_loc[r2 * usz // 1024:(r2 + 1) * usz // 1024, :]],
                outs=[wT_gus[r2]],
                replica_groups=[list(range(NC))]).then_inc(S_["ccs"].h, 1))
        prep_T(wdnnat, E, FSL, up_sz, nw_dn)
        wait("g", S_["wtA"], S_["wtA"].n)
        t_ccwd = S_["ccs"].tok()
        on("g", lambda eng: eng.collective_compute(
            "AllGather", ALU.bypass,
            ins=[wT_loc[up_sz // 1024:(up_sz + dn_sz) // 1024, :]], outs=[wT_gd],
            replica_groups=[list(range(NC))]).then_inc(S_["ccs"].h, 1))
        prep_T(wqknat, 3 * E, ESL, up_sz + dn_sz, nw_cast)
        prep_T(wprnat, E, ESL, up_sz + dn_sz + qk_sz, nw_cast)
        ST["prep_v_done"] = eV.n
        wait("g", S_["wtA"], S_["wtA"].n)
        t_ccw2 = S_["ccs"].tok()
        on("g", lambda eng: eng.collective_compute(
            "AllGather", ALU.bypass,
            ins=[wT_loc[(up_sz + dn_sz) // 1024:, :]], outs=[wT_gr],
            replica_groups=[list(range(NC))]).then_inc(S_["ccs"].h, 1))

        def emit_ln(tag, t_x_ready_v):
            tv_xb, t_sq, t_sqmm = {}, {}, {}
            xsq = [cbig[:, ln_xsq:ln_xsq + TPC], cbig[:, ln_mbc:ln_mbc + TPC]]
            nh = TPC // CW
            SOFF = max(TPC, 512)
            for dc in range(NDC):
                slot = dc % 2
                wait("v", eV, t_x_ready_v)
                if tag == "ln1" and dc == 0:
                    wait("v", S_["dXL"], t_xl)
                tv_xb[dc] = op("v", eV, "tensor_copy",
                               out=xn_sb[:, dc * TPC:(dc + 1) * TPC],
                               in_=x_sb[:, dc * TPC:(dc + 1) * TPC])
                wait("s", eV, tv_xb[dc])
                if dc >= 2:
                    wait("s", eT, t_sqmm[dc - 2])
                elif tag != "ln1":
                    wait("s", eT, ST.get(("lnfence", tag), 0))
                t_sq[dc] = op("s", eS, "activation", out=xsq[slot],
                              in_=xn_sb[:, dc * TPC:(dc + 1) * TPC], func=AF.Square)
                wait("t", eV, tv_xb[dc])
                if dc == 0:
                    wait("t", eV, ST.get(("vfence", tag), 0))
                    wait("t", eS, ST.get(("sfence", tag), 0))
                for h_ in range(nh):
                    mm(PS[0:1, h_ * CW:(h_ + 1) * CW], ones_col[:],
                       xn_sb[:, dc * TPC + h_ * CW:dc * TPC + (h_ + 1) * CW],
                       start=(dc == 0), stop=(dc == NDC - 1))
                wait("t", eS, t_sq[dc])
                last = None
                for h_ in range(nh):
                    last = mm(PS[0:1, SOFF + h_ * CW:SOFF + (h_ + 1) * CW], ones_col[:],
                              xsq[slot][:, h_ * CW:(h_ + 1) * CW],
                              start=(dc == 0), stop=(dc == NDC - 1),
                              inc=eT if h_ == nh - 1 else None)
                t_sqmm[dc] = last
            t_stats = t_sqmm[NDC - 1]
            nmrow = cbig[0:1, ln_rows:ln_rows + TPC]
            msrow = cbig[0:1, ln_rows + TPC:ln_rows + 2 * TPC]
            wait("v", eT, t_stats)
            t1 = op("v", eV, "tensor_scalar", out=nmrow, in0=PS[0:1, 0:TPC],
                    scalar1=-1.0 / E, scalar2=None, op0=ALU.mult)
            wait("v", eV, t1)
            t2 = op("v", eV, "tensor_mul", out=msrow, in0=nmrow, in1=nmrow)
            wait("v", eV, t2)
            t3 = op("v", eV, "scalar_tensor_tensor", out=msrow, in0=PS[0:1, SOFF:SOFF + TPC],
                    scalar=1.0 / E, in1=msrow, op0=ALU.mult, op1=ALU.subtract)
            wait("v", eV, t3)
            t4 = op("v", eV, "tensor_scalar", out=msrow, in0=msrow,
                    scalar1=1e-5, scalar2=None, op0=ALU.add)
            wait("v", eV, t4)
            t5 = op("v", eV, "reciprocal", out=msrow, in_=msrow, _lp=True)
            wait("s", eV, t5)
            t6 = op("s", eS, "activation", out=msrow, in_=msrow, func=AF.Sqrt)
            wait("t", eV, t5)
            for h_ in range(nh):
                mm(PS[:, SOFF + h_ * CW:SOFF + (h_ + 1) * CW], ones_bf[:],
                   nmrow[:, h_ * CW:(h_ + 1) * CW], start=True, stop=True)
            wait("t", eS, t6)
            t_bc = None
            for h_ in range(nh):
                t_bc = mm(PS[:, h_ * CW:(h_ + 1) * CW], ones_bf[:],
                          msrow[:, h_ * CW:(h_ + 1) * CW], start=True, stop=True,
                          inc=eT if h_ == nh - 1 else None)
            wait("v", eT, t_bc)
            op("v", eV, "tensor_copy", out=cbig[:, ln_mbc:ln_mbc + TPC], in_=PS[:, SOFF:SOFF + TPC])
            t8 = op("v", eV, "tensor_copy", out=cbig[:, ln_rbc:ln_rbc + TPC], in_=PS[:, 0:TPC])
            tv_xn = {}
            for dc in range(NDC):
                wait("v", eV, t8)
                ta = op("v", eV, "tensor_add",
                        out=xn_sb[:, dc * TPC:(dc + 1) * TPC],
                        in0=x_sb[:, dc * TPC:(dc + 1) * TPC],
                        in1=cbig[:, ln_mbc:ln_mbc + TPC])
                wait("v", eV, ta)
                tv_xn[dc] = op("v", eV, "tensor_mul",
                               out=xn_sb[:, dc * TPC:(dc + 1) * TPC],
                               in0=xn_sb[:, dc * TPC:(dc + 1) * TPC],
                               in1=cbig[:, ln_rbc:ln_rbc + TPC])
            return tv_xn

        def emit_ffn(tag, tv_xn, first):
            t_res = {}
            for tt in range(NTT):
                tok0 = tt * TTOK
                u32 = stage[0:32, 0:TTOK]
                v32 = stage[0:32, TTOK:2 * TTOK]
                wait("g", eV, eV.n)
                wait("g", eS, eS.n)
                wait("g", eT, eT.n)
                tz2 = op("g", eG, "memset", ap=stage[0:32, 0:2 * TTOK], constant=0.0)
                tz1 = op("g", eG, "memset", ap=fbig[0:32, 0:F], constant=0.0)
                wait("g", eG, tz1)
                tb32 = dma("g", S_["b32s"], fbig[0:16, 0:F], bupl)
                ubank = PSC[0:16, 0:TTOK]
                tu = None
                for dc in range(NDC):
                    wait("t", eV, tv_xn[dc])
                    if dc == 0:
                        wait("t", S_["dLOR"], t_adn)
                    tu = mm(ubank, aup_sb[:, dc * R:(dc + 1) * R],
                            xn_sb[:, dc * TPC + tok0:dc * TPC + tok0 + TTOK],
                            start=(dc == 0), stop=(dc == NDC - 1),
                            inc=eT if dc == NDC - 1 else None)
                wait("v", eT, tu)
                wait("v", eG, tz2)
                tu32 = op("v", eV, "tensor_copy", out=u32[0:16, :], in_=ubank)
                t_x2, up_last = {}, {}
                tld = {0: None, 1: None}
                UQ = NHC // NUS
                for pre in range(min(2, NHC)):
                    psl = pre % 2
                    if first and tt == 0:
                        wait("p", S_["ccs"], t_ccwU[pre // UQ])
                    wait("p", eT, eT.n)
                    tld[psl] = wblk(S_["wsA" if psl == 0 else "wsB"],
                                    gbuf[:, psl * NDC * 128:(psl + 1) * NDC * 128],
                                    upTs[pre // UQ], (pre % UQ) * 128, ESL // 128)
                t_vT = None
                for hc in range(NHC):
                    slot = hc % 2
                    wsem = S_["wsA" if slot == 0 else "wsB"]
                    wait("t", wsem, tld[slot])
                    if hc == 0:
                        wait("t", eV, tu32)
                        wait("t", S_["b32s"], tb32)
                        wait("t", eG, tz1)
                        wait("t", eV, t_bias_done)
                    pb = PS[:, slot * 512:slot * 512 + TTOK]
                    if hc >= 2:
                        wait("t", eS, t_x2[hc - 2])
                    for dc in range(NDC):
                        mm(pb, gbuf[:, slot * NDC * 128 + dc * 128:slot * NDC * 128 + (dc + 1) * 128],
                           xn_sb[:, dc * TPC + tok0:dc * TPC + tok0 + TTOK],
                           start=(dc == 0), stop=False)
                    up_last[hc] = mm(pb, fbig[0:32, hc * 128:(hc + 1) * 128], u32,
                                     start=False, stop=True, inc=eT)
                    nxt = hc + 2
                    if nxt < NHC:
                        wait("p", eT, up_last[hc])
                        if first and tt == 0 and nxt % UQ in (0, 1) and nxt >= UQ:
                            wait("p", S_["ccs"], t_ccwU[nxt // UQ])
                        tld[slot] = wblk(wsem,
                                         gbuf[:, slot * NDC * 128:(slot + 1) * NDC * 128],
                                         upTs[nxt // UQ], (nxt % UQ) * 128, ESL // 128)
                    wait("s", eT, up_last[hc])
                    if (not first) and tt == 0 and hc == 0:
                        for s2 in ("kvA", "kvB", "kvC", "vvA", "vvB",
                                   "qA", "qB", "qC"):
                            wait("s", S_[s2], S_[s2].n)
                        wait("s", eV, eV.n)
                    if first and tt == 0 and hc == 0:
                        wait("s", eV, ST["prep_dn_v"])
                        for i_ in range(NPQ):
                            wait("s", S_[f"ns{i_}"], ST["prep_dn_ns"][i_])
                    if first and tt == 0 and hc == tr0 // TTOK:
                        wait("s", S_["trA"], S_["trA"].n)
                        wait("s", S_["wtA"], S_["wtA"].n)
                    t_x2[hc] = op("s", eS, "activation",
                                  out=cbig[:, hc * TTOK:(hc + 1) * TTOK], in_=pb,
                                  func=AF.Relu, bias=bupc[:, hc:hc + 1], scale=1.0)
                    vbank = PSC[0:16, 0:TTOK]
                    wait("t", eS, t_x2[hc])
                    if hc == 0:
                        wait("t", eV, tu32)
                    t_vT = mm(vbank, adn_sb[:, hc * R:(hc + 1) * R],
                              cbig[:, hc * TTOK:(hc + 1) * TTOK],
                              start=(hc == 0), stop=(hc == NHC - 1),
                              inc=eT if hc == NHC - 1 else None)
                wait("v", eT, t_vT)
                tv32 = op("v", eV, "tensor_copy", out=v32[0:16, :], in_=vbank)
                wait("g", eT, t_vT)
                tzg = op("g", eG, "memset", ap=gbuf[0:32, 0:E], constant=0.0)
                wait("g", eG, tzg)
                tbdn = dma("g", S_["b32s"], gbuf[0:16, 0:E], bdnl)
                tldd = {0: None, 1: None}
                for pre in range(min(2, NDC)):
                    psl = pre % 2
                    wait("p", eT, t_vT)
                    if first and tt == 0:
                        wait("p", S_["ccs"], t_ccwd)
                    tldd[psl] = wblk(S_["wsA" if psl == 0 else "wsB"],
                                     fbig[:, psl * F:psl * F + F],
                                     dnT, pre * 128, FSL // 128)
                dn_last, t_rs = {}, {}
                for dc in range(NDC):
                    slot = dc % 2
                    wsem = S_["wsA" if slot == 0 else "wsB"]
                    wait("t", wsem, tldd[slot])
                    if dc == 0:
                        wait("t", eV, tv32)
                        wait("t", S_["b32s"], tbdn)
                        wait("t", eG, tzg)
                    pb = PS[:, 1024 + slot * 512:1024 + slot * 512 + TTOK]
                    if dc >= 2:
                        wait("t", eS, t_rs[dc - 2])
                    for hc in range(NHC):
                        mm(pb, fbig[:, slot * F + hc * 128:slot * F + (hc + 1) * 128],
                           cbig[:, hc * TTOK:(hc + 1) * TTOK],
                           start=(hc == 0), stop=False)
                    dn_last[dc] = mm(pb, gbuf[0:32, dc * 128:(dc + 1) * 128], v32,
                                     start=False, stop=True, inc=eT)
                    nxt = dc + 2
                    if nxt < NDC:
                        wait("p", eT, dn_last[dc])
                        tldd[slot] = wblk(wsem,
                                          fbig[:, slot * F:slot * F + F],
                                          dnT, nxt * 128, FSL // 128)
                    ystage = stage[:, TTOK * 2 + (dc % 2) * TTOK:TTOK * 2 + (dc % 2 + 1) * TTOK]
                    wait("s", eT, dn_last[dc])
                    prevy = ST.get(("ystage", dc % 2))
                    if prevy is not None:
                        wait("s", eV, prevy)
                    t_rs[dc] = op("s", eS, "activation", out=ystage, in_=pb,
                                  func=AF.Identity, bias=bdnc[:, dc:dc + 1], scale=0.5)
                    wait("v", eS, t_rs[dc])
                    t_res[dc] = op("v", eV, "tensor_add",
                                   out=x_sb[:, dc * TPC + tok0:dc * TPC + tok0 + TTOK],
                                   in0=x_sb[:, dc * TPC + tok0:dc * TPC + tok0 + TTOK],
                                   in1=ystage)
                    ST[("ystage", dc % 2)] = t_res[dc]
            return t_res

        PHASE = cfg.get("phase", "all")
        tv_xn1 = emit_ln("ln1", 0)
        if PHASE == "ln1x":
            t_out = None
            for dc in range(NDC):
                wait("g", eV, tv_xn1[dc])
                t_out = dma("g", S_["dOUT"], y[dc * 128:(dc + 1) * 128, :],
                            xn_sb[:, dc * TPC:(dc + 1) * TPC])
            wait("g", S_["dOUT"], t_out)
            return _finish(nc, en, Q)
        t_res1 = emit_ffn("ffn1", tv_xn1, first=True)
        if PHASE == "up1":
            t_out = None
            wait("g", eS, eS.n)
            for hc in range(min(NHC, NDC)):
                t_out = dma("g", S_["dOUT"], y[hc * 128:(hc + 1) * 128, 0:TTOK],
                            cbig[:, hc * TTOK:(hc + 1) * TTOK])
            wait("g", S_["dOUT"], t_out)
            return _finish(nc, en, Q)
        if PHASE == "ffn1":
            t_out = None
            for dc in range(NDC):
                wait("g", eV, t_res1[dc])
                t_out = dma("g", S_["dOUT"], y[dc * 128:(dc + 1) * 128, :],
                            x_sb[:, dc * TPC:(dc + 1) * TPC])
            wait("g", S_["dOUT"], t_out)
            return _finish(nc, en, Q)
        ST[("vfence", "ln2")] = eV.n
        ST[("sfence", "ln2")] = eS.n
        ST[("lnfence", "ln2")] = eT.n
        tv_xn2 = emit_ln("ln2", t_res1[NDC - 1])

        t_kvst, t_qst = None, None
        t_evq, qkv_last = {}, {}
        tqld = {0: None, 1: None}
        for pre in range(2):
            psl = pre % 2
            wait("p", eT, eT.n)
            wait("p", S_["ccs"], t_ccw2)
            tqld[psl] = wblk(S_["wsA" if psl == 0 else "wsB"],
                             gbuf[:, psl * NDC * 128:(psl + 1) * NDC * 128],
                             qkT, pre * 128, ESL // 128)
        for jc in range(NJC):
            slot = jc % 2
            wsem = S_["wsA" if slot == 0 else "wsB"]
            wait("t", wsem, tqld[slot])
            pb = PS[:, slot * 1024:slot * 1024 + TPC]
            if jc >= 2:
                wait("t", eS, t_evq[jc - 2])
            elif jc == 0:
                wait("t", eV, eV.n)
                wait("t", eS, eS.n)
            for dc in range(NDC):
                if jc == 0:
                    wait("t", eV, tv_xn2[dc])
                for h_ in range(TPC // CW):
                    qkv_last[jc] = mm(pb[:, h_ * CW:(h_ + 1) * CW],
                                      gbuf[:, slot * NDC * 128 + dc * 128:slot * NDC * 128 + (dc + 1) * 128],
                                      xn_sb[:, dc * TPC + h_ * CW:dc * TPC + (h_ + 1) * CW],
                                      start=(dc == 0), stop=(dc == NDC - 1),
                                      inc=eT if (dc == NDC - 1 and h_ == TPC // CW - 1) else None)
            nxt = jc + 2
            if nxt < NJC:
                wait("p", eT, qkv_last[jc])
                tqld[slot] = wblk(wsem,
                                  gbuf[:, slot * NDC * 128:(slot + 1) * NDC * 128],
                                  qkT, nxt * 128, ESL // 128)
            h = jc // 3
            s_ = jc % 3
            sslot = jc % 2
            sls = [S_["dST0"], S_["dST1"]]
            dst = stage[:, sslot * TPC:(sslot + 1) * TPC]
            wait("s", eT, qkv_last[jc])
            wait("s", sls[sslot], sls[sslot].n)
            t_evq[jc] = op("s", eS, "activation", out=dst, in_=pb,
                           func=AF.Identity, bias=bqkc[:, jc:jc + 1],
                           scale=float(128.0 ** 0.5) if s_ == 0 else 1.0)
            if s_ == 0:
                wait("g", eS, t_evq[jc])
                t_qst = dma("g", sls[sslot], q_loc[h], dst)
            elif s_ == 1:
                wait("g", eS, t_evq[jc])
                kbase, _ = kv_kbase(h)
                t_kvst = dma("g", sls[sslot],
                             kvf[kbase:kbase + 128 * TPC].rearrange("(p cc) -> p cc", p=128),
                             dst)
            elif PHASE == "vfm":
                wait("g", eS, t_evq[jc])
                vbase, _ = kv_vbase(h)
                t_kvst = dma("g", sls[sslot],
                             kvf[vbase:vbase + TPC * 128].rearrange("(p cc) -> p cc", p=128),
                             dst)
            else:
                ntile = TPC // 128
                vps = PSB[:, 0:TPC]
                t_tr = None
                for tc in range(ntile):
                    wait("t", eS, t_evq[jc])
                    if tc == 0 and ST.get("vtr_ev") is not None:
                        wait("t", eV, ST["vtr_ev"])
                    t_tr = tpose(vps[:, tc * 128:(tc + 1) * 128], dst[:, tc * 128:(tc + 1) * 128],
                                 inc=eT if tc == ntile - 1 else None)
                oslot = 1 - sslot
                wait("v", eT, t_tr)
                wait("v", sls[oslot], sls[oslot].n)
                vdst = stage[:, oslot * TPC:(oslot + 1) * TPC]
                tev = op("v", eV, "tensor_copy", out=vdst, in_=vps)
                ST["vtr_ev"] = tev
                wait("g", eV, tev)
                vbase, _ = kv_vbase(h)
                t_kvst = dma("g", sls[oslot],
                             kvf[vbase:vbase + TPC * 128].rearrange(
                                 "(tc p f) -> p tc f", p=128, f=128),
                             vdst.rearrange("p (tc f) -> p tc f", f=128))
            if jc == 3 * HG0 - 1 and PHASE not in ("qx", "kx", "vx", "vfm"):
                wait("g", S_["dST0"], S_["dST0"].n)
                wait("g", S_["dST1"], S_["dST1"].n)
                t_cckv = S_["ccs"].tok()
                on("g", lambda eng: eng.collective_compute(
                    "AllGather", ALU.bypass, ins=[kv_loc[0:GH0 // 1024, :]], outs=[kv_g0],
                    replica_groups=[list(range(NC))]).then_inc(S_["ccs"].h, 1))

        wait("g", S_["dST0"], S_["dST0"].n)
        wait("g", S_["dST1"], S_["dST1"].n)
        if PHASE in ("qx", "kx", "vx", "vfm"):
            t_out = None
            for h_ in range(H):
                if PHASE == "qx":
                    t_out = dma("g", S_["dOUT"], y[h_ * 128:(h_ + 1) * 128, :], q_loc[h_])
                elif PHASE == "kx":
                    kb2, _ = kv_kbase(h_)
                    t_out = dma("g", S_["dOUT"], y[h_ * 128:(h_ + 1) * 128, :],
                                kvf[kb2:kb2 + 128 * TPC].rearrange(
                                    "(p cc) -> p cc", p=128))
                elif PHASE == "vfm":
                    vb2, _ = kv_vbase(h_)
                    t_out = dma("g", S_["dOUT"], y[h_ * 128:(h_ + 1) * 128, :],
                                kvf[vb2:vb2 + TPC * 128].rearrange(
                                    "(p cc) -> p cc", p=128))
                else:
                    vb2, _ = kv_vbase(h_)
                    t_out = dma("g", S_["dOUT"], y[h_ * 128:(h_ + 1) * 128, :],
                                kvf[vb2:vb2 + TPC * 128].rearrange(
                                    "(l f) -> f l", f=128))
            wait("g", S_["dOUT"], t_out)
            return _finish(nc, en, Q)
        if PHASE in ("qx", "kx", "vx", "vfm"):
            t_cckv = S_["ccs"].tok()
            on("g", lambda eng: eng.collective_compute(
                "AllGather", ALU.bypass, ins=[kv_loc[0:GH0 // 1024, :]], outs=[kv_g0],
                replica_groups=[list(range(NC))]).then_inc(S_["ccs"].h, 1))
        t_cckv2 = S_["ccs"].tok()
        on("g", lambda eng: eng.collective_compute(
            "AllGather", ALU.bypass, ins=[kv_loc[GH0 // 1024:, :]], outs=[kv_g1],
            replica_groups=[list(range(NC))]).then_inc(S_["ccs"].h, 1))

        kslot = [xn_sb[:, 0:S], xn_sb[:, S:2 * S], xn_sb[:, 2 * S:3 * S]]
        vslot = [xn_sb[:, 3 * S:4 * S], cbig[:, 0:S]]
        at0 = S
        atslot = [cbig[:, at0 + i * QL:at0 + (i + 1) * QL] for i in range(3)]
        mrow = [cbig[0:1, at0 + (3 + i) * QL:at0 + (4 + i) * QL] for i in range(2)]
        rsrow = [cbig[0:1, at0 + (5 + i) * QL:at0 + (6 + i) * QL] for i in range(2)]
        qslot = [cbig[:, at0 + (7 + i) * QL:at0 + (8 + i) * QL] for i in range(3)]
        mcol = cbig[:, at0 + 10 * QL:at0 + 10 * QL + 8]
        rsb = [cbig[:, at0 + 10 * QL + 16 + i * QL:at0 + 10 * QL + 16 + (i + 1) * QL]
               for i in range(2)]
        accsb = cbig[:, at0 + 10 * QL + 16 + 2 * QL:at0 + 10 * QL + 16 + 3 * QL]
        KHs = [HG0 * 128 * TPC, HG1 * 128 * TPC]
        kKs = [kvgs[g][:, 0:KHs[g]].rearrange("r (d l) -> d r l", l=TPC) for g in range(2)]
        kVs = [kvgs[g][:, KHs[g]:].rearrange("r (h l f) -> h r l f", l=TPC, f=128) for g in range(2)]

        useq = [(b_, h_) for h_ in range(HG0) for b_ in range(B)] + \
               [(b_, h_) for h_ in range(HG0, H) for b_ in range(B)]
        NB1 = S // 512
        NP1 = NQT * NB1
        ksem = ["kvA", "kvB", "kvC"]
        qsem = ["qA", "qB", "qC"]

        def ccwait(ui):
            g = kv_group(useq[ui][1])
            wait("p", S_["ccs"], t_cckv2 if g == 1 else t_cckv)

        def loadK(ui):
            b_, h_ = useq[ui]
            g = kv_group(h_)
            hl = h_ - (0 if g == 0 else HG0)
            ccwait(ui)
            return dma("p", S_[ksem[ui % 3]],
                       kslot[ui % 3].rearrange("p (r cc) -> p r cc", r=NC),
                       kKs[g][hl * 128:(hl + 1) * 128, :, b_ * QL:(b_ + 1) * QL])

        def loadQ(ui):
            b_, h_ = useq[ui]
            ccwait(ui)
            return dma("p", S_[qsem[ui % 3]],
                       qslot[ui % 3], q_loc[h_][:, b_ * QL:(b_ + 1) * QL])

        def loadV(ui):
            b_, h_ = useq[ui]
            g = kv_group(h_)
            hl = h_ - (0 if g == 0 else HG0)
            ccwait(ui)
            vd4 = vslot[ui % 2].rearrange("p (r sb f) -> p r sb f", r=NC, f=128)
            vs4 = kVs[g][hl, :, b_ * QL:(b_ + 1) * QL, :].rearrange(
                "r (sb p) f -> p r sb f", p=128)
            t = None
            for sb in range(QL // 128):
                t = dma("p", S_["vvA" if ui % 2 == 0 else "vvB"],
                        vd4[:, :, sb, :], vs4[:, :, sb, :])
            return t

        def uu_of(w):
            return useq[w][0] * H + useq[w][1]

        t_o_ev = {}
        kq_tok, vv_tok = {}, {}
        wait("g", eT, eT.n)
        wait("g", S_["dST0"], S_["dST0"].n)
        wait("g", S_["dST1"], S_["dST1"].n)
        tmz = op("g", eG, "memset", ap=cbig[:, at0 + 3 * QL:at0 + 5 * QL],
                 constant=0.0)
        tmz2 = op("g", eG, "memset", ap=stage[:, 0:128], constant=0.0)
        ST["mz"] = tmz
        wait("s", eG, tmz2)
        wait("s", eV, eV.n)
        op("s", eS, "copy", out=stage[0:1, 0:128], in_=ones_bf[:])
        mrowpad = [cbig[:, at0 + (3 + i) * QL:at0 + (4 + i) * QL] for i in range(2)]

        def p1_item(u_, j):
            qt, kb = j // NB1, j % NB1
            bk = j % 2
            pba = PS[:, bk * 512:bk * 512 + 512]
            prev = ST.get(("p1bank", bk))
            if prev is not None:
                wait("t", eV, prev)
            t_mm = mm(pba, qslot[u_ % 3][:, qt * QTW:(qt + 1) * QTW],
                      kslot[u_ % 3][:, kb * 512:(kb + 1) * 512],
                      start=True, stop=True, inc=eT)
            wait("v", eT, t_mm)
            t_mx = op("v", eV, "tensor_reduce", out=sm[:, j:j + 1], in_=pba,
                      axis=AX.X, op=ALU.max, negate=True)
            ST[("p1bank", bk)] = t_mx
            if kb == NB1 - 1:
                wait("v", eV, t_mx)
                if NB1 > 1:
                    t_cb = op("v", eV, "tensor_reduce", out=sm[:, 32 + qt:33 + qt],
                              in_=sm[:, qt * NB1:(qt + 1) * NB1], axis=AX.X, op=ALU.min)
                else:
                    t_cb = op("v", eV, "tensor_copy", out=sm[:, 32 + qt:33 + qt],
                              in_=sm[:, j:j + 1])
                prev_t = ST.get(("mtr", qt))
                if prev_t is not None:
                    wait("v", eT, prev_t)
                wait("v", eV, t_cb)
                ST[("mcol", qt)] = op("v", eV, "tensor_copy", _lp=True,
                                      out=mcol[:, qt:qt + 1],
                                      in_=sm[:, 32 + qt:33 + qt])

        def p1_tail(u_):
            t_tr = None
            for qt in range(NQT):
                wait("t", eV, ST[("mcol", qt)])
                if qt == 0:
                    prev_m = ST.get(("mrow", u_ - 1))
                    if prev_m is not None:
                        wait("t", eS, prev_m)
                t_tr = tpose(PSB[0:1, qt * QTW:(qt + 1) * QTW],
                             mcol[:, qt:qt + 1], inc=eT)
                ST[("mtr", qt)] = t_tr
            prev_b = ST.get(("p2mm", u_ - 2))
            if prev_b is not None:
                wait("s", eT, prev_b)
            wait("s", eT, t_tr)
            ST[("mrow", u_)] = op("s", eS, "copy", _lp=True, out=mrow[u_ % 2],
                                  in_=PSB[0:1, 0:QL])

        def p2_consume(w, kc):
            sl = kc % 3
            obw = 4 + (w % 2)
            ob_ap = PS[:, obw * 512:obw * 512 + QL]
            t_exp = ST[("pexp", w)][kc]
            wait("t", eS, t_exp)
            if kc == 0:
                wait("t", S_["vvA" if w % 2 == 0 else "vvB"], vv_tok[w])
                prev_o = ST.get(("oev", w - 1))
                if prev_o is not None:
                    wait("t", eV, prev_o)
            t_av = mm(ob_ap, vslot[w % 2][:, kc * 128:(kc + 1) * 128], atslot[sl],
                      start=(kc == 0), stop=(kc == NKC - 1), inc=eT)
            wait("v", eS, t_exp)
            if kc == 0:
                prev_pc = ST.get(("rscp", w - 1))
                if prev_pc is not None:
                    wait("v", eS, prev_pc)
                t_ac = op("v", eV, "tensor_copy", out=PSC[:, 0:QL], in_=atslot[sl])
            else:
                wait("v", eV, ST[("acc", w)])
                t_ac = op("v", eV, "tensor_add", out=PSC[:, 0:QL],
                          in0=PSC[:, 0:QL], in1=atslot[sl])
            ST[("acc", w)] = t_ac
            ST[("atrd", sl)] = (t_av, t_ac)
            if kc == NKC - 1:
                ST[("av", w)] = t_av

        def p2_item(w, kc):
            pb = 2 + (kc % 2)
            pba = PS[:, pb * 512:pb * 512 + QL]
            prev_exp = ST.get(("p2bank", pb))
            if prev_exp is not None:
                wait("t", eS, prev_exp)
            if kc == 0:
                wait("t", eS, ST[("mrow", w)])
                if w == 0:
                    wait("t", eG, ST["mz"])
                ST[("pexp", w)] = {}
            mm(pba, stage[:, 0:128], mrowpad[w % 2], start=True, stop=False)
            t_sc = mm(pba, kslot[w % 3][:, kc * 128:(kc + 1) * 128], qslot[w % 3],
                      start=False, stop=True, inc=eT)
            ST[("p2mm", w)] = t_sc
            sl = kc % 3
            wait("s", eT, t_sc)
            prev_rd = ST.get(("atrd", sl))
            if prev_rd is not None:
                wait("s", eT, prev_rd[0])
                wait("s", eV, prev_rd[1])
            t_exp = op("s", eS, "activation", _lp=True, out=atslot[sl], in_=pba,
                       func=AF.Exp, bias=0.0, scale=1.0)
            ST[("pexp", w)][kc] = t_exp
            ST[("p2bank", pb)] = t_exp
            if kc >= 1:
                p2_consume(w, kc - 1)

        def p2_tail(w):
            ww = uu_of(w)
            p2_consume(w, NKC - 1)
            wait("s", eV, ST[("acc", w)])
            prev_on = ST.get(("ones", w - 1))
            if prev_on is not None:
                wait("s", eT, prev_on)
            t_acp = op("s", eS, "copy", _lp=True, out=accsb, in_=PSC[:, 0:QL])
            wait("t", eS, t_acp)
            t_ones = mm(PSC[0:1, 0:QL], ones_col[:], accsb, start=True, stop=True,
                        inc=eT)
            ST[("ones", w)] = t_ones
            wait("v", eT, t_ones)
            prev_rs = ST.get(("rsmm", w - 2))
            if prev_rs is not None:
                wait("v", eT, prev_rs)
            t_rc = op("v", eV, "reciprocal", _lp=True, out=rsrow[w % 2],
                      in_=PSC[0:1, 0:QL])
            wait("t", eV, t_rc)
            t_rsmm = mm(PSC[:, 0:QL], ones_bf[:], rsrow[w % 2], start=True, stop=True,
                        inc=eT)
            ST[("rsmm", w)] = t_rsmm
            wait("s", eT, t_rsmm)
            prev_rb = ST.get(("rsb", w - 2))
            if prev_rb is not None:
                wait("s", eV, prev_rb)
            t_rsc = op("s", eS, "copy", _lp=True, out=rsb[w % 2], in_=PSC[:, 0:QL])
            ST[("rscp", w)] = t_rsc
            wait("v", eS, t_rsc)
            obw = 4 + (w % 2)
            t_o_ev[ww] = op("v", eV, "tensor_mul", _lp=True,
                            out=fbig[:, ww * QL:(ww + 1) * QL],
                            in0=PS[:, obw * 512:obw * 512 + QL], in1=rsb[w % 2])
            ST[("oev", w)] = t_o_ev[ww]
            ST[("rsb", w)] = t_o_ev[ww]

        NS = max(NP1, NKC)
        for u in range(NU + 1):
            if u == 0:
                wait("p", eT, eT.n)
                for pre in range(min(3, NU)):
                    kq_tok[pre] = (loadK(pre), loadQ(pre))
                for pre in range(min(2, NU)):
                    vv_tok[pre] = loadV(pre)
            if u < NU:
                ktok, qtok = kq_tok[u]
                wait("t", S_[ksem[u % 3]], ktok)
                wait("t", S_[qsem[u % 3]], qtok)
                if u == 0:
                    wait("t", eV, eV.n)
                    wait("t", eS, eS.n)
            HS = max(1, NS * 2 // 3)
            for j in range(NS):
                if u < NU:
                    for i1 in range(min(j, HS) * NP1 // HS,
                                    min(j + 1, HS) * NP1 // HS):
                        p1_item(u, i1)
                    if j + 1 == HS:
                        p1_tail(u)
                if u >= 1:
                    for i2 in range(j * NKC // NS, (j + 1) * NKC // NS):
                        p2_item(u - 1, i2)
            if u < NU and NS < max(1, HS):
                p1_tail(u)
            if u >= 1:
                p2_tail(u - 1)
                if u + 1 < NU and u + 1 >= 2:
                    wait("p", eT, ST[("av", u - 1)])
                    vv_tok[u + 1] = loadV(u + 1)
                if u + 2 < NU:
                    wait("p", eT, ST[("p2mm", u - 1)])
                    kq_tok[u + 2] = (loadK(u + 2), loadQ(u + 2))

        if PHASE == "ox":
            t_out = None
            for u in range(NU):
                b_, h_ = u // H, u % H
                wait("g", eV, t_o_ev[u])
                t_out = dma("g", S_["dOUT"],
                            y[h_ * 128:(h_ + 1) * 128, b_ * QL:(b_ + 1) * QL],
                            fbig[:, u * QL:(u + 1) * QL])
            wait("g", S_["dOUT"], t_out)
            return _finish(nc, en, Q)
        tpld = {0: None, 1: None}
        for pre in range(min(2, NDC)):
            psl = pre % 2
            wait("p", eT, qkv_last[NJC - 1])
            tpld[psl] = wblk(S_["wsA" if psl == 0 else "wsB"],
                             gbuf[:, psl * NDC * 128:(psl + 1) * NDC * 128],
                             prT, pre * 128, ESL // 128)
        pr_last, t_evp, t_res2 = {}, {}, {}
        for dc in range(NDC):
            slot = dc % 2
            wsem = S_["wsA" if slot == 0 else "wsB"]
            wait("t", wsem, tpld[slot])
            for b_ in range(B):
                pbi = 2 * (dc % 2) + b_ if B > 1 else dc % 2
                pb = PS[:, pbi * 512:pbi * 512 + QL]
                if (dc - 2, b_) in t_evp:
                    wait("t", eS, t_evp[(dc - 2, b_)])
                elif dc == 0 and b_ == 0:
                    wait("t", eV, eV.n)
                    wait("t", eS, eS.n)
                for ec in range(NDC):
                    uu = b_ * H + ec
                    if dc == 0:
                        wait("t", eV, t_o_ev[uu])
                    pr_last[(dc, b_)] = mm(
                        pb, gbuf[:, slot * NDC * 128 + ec * 128:slot * NDC * 128 + (ec + 1) * 128],
                        fbig[:, uu * QL:(uu + 1) * QL],
                        start=(ec == 0), stop=(ec == NDC - 1), inc=eT if ec == NDC - 1 else None)
                wait("s", eT, pr_last[(dc, b_)])
                sl = pbi % 2
                zst = stage[:, sl * TPC:sl * TPC + QL]
                prevz = ST.get(("zst", sl))
                if prevz is not None:
                    wait("s", eV, prevz)
                else:
                    wait("s", S_["dST0"], S_["dST0"].n)
                    wait("s", S_["dST1"], S_["dST1"].n)
                t_evp[(dc, b_)] = op("s", eS, "activation", out=zst, in_=pb,
                                     func=AF.Identity, bias=bprc[:, dc:dc + 1], scale=1.0)
                wait("v", eS, t_evp[(dc, b_)])
                t_res2[dc] = op("v", eV, "tensor_add",
                                out=x_sb[:, dc * TPC + b_ * QL:dc * TPC + (b_ + 1) * QL],
                                in0=x_sb[:, dc * TPC + b_ * QL:dc * TPC + (b_ + 1) * QL],
                                in1=zst)
                ST[("zst", sl)] = t_res2[dc]
            nxt = dc + 2
            if nxt < NDC:
                wait("p", eT, pr_last[(dc, B - 1)])
                tpld[slot] = wblk(wsem,
                                  gbuf[:, slot * NDC * 128:(slot + 1) * NDC * 128],
                                  prT, nxt * 128, ESL // 128)

        if PHASE == "attn":
            t_out = None
            for dc in range(NDC):
                wait("g", eV, t_res2[dc])
                t_out = dma("g", S_["dOUT"], y[dc * 128:(dc + 1) * 128, :],
                            x_sb[:, dc * TPC:(dc + 1) * TPC])
            wait("g", S_["dOUT"], t_out)
            return _finish(nc, en, Q)
        ST[("vfence", "ln3")] = eV.n
        ST[("sfence", "ln3")] = eS.n
        ST[("lnfence", "ln3")] = eT.n
        tv_xn3 = emit_ln("ln3", t_res2[NDC - 1])
        t_res3 = emit_ffn("ffn2", tv_xn3, first=False)

        t_out = None
        for dc in range(NDC):
            wait("g", eV, t_res3[dc])
            t_out = dma("g", S_["dOUT"], y[dc * 128:(dc + 1) * 128, :],
                        x_sb[:, dc * TPC:(dc + 1) * TPC])
        wait("g", S_["dOUT"], t_out)

        return _finish(nc, en, Q)


def _finish(nc, en, Q):
    block = en(nc.Block())

    @block.tensor
    def _(eng):
        for f in Q["t"]:
            f(eng)

    @block.vector
    def _(eng):
        for f in Q["v"]:
            f(eng)

    @block.scalar
    def _(eng):
        for f in Q["s"]:
            f(eng)

    @block.sync
    def _(eng):
        for f in Q["p"]:
            f(eng)

    @block.gpsimd
    def _(eng):
        for f in Q["g"]:
            f(eng)

    return nc



def make_in_maps(cfg, I):
    c = derive(cfg)
    B, S, E = c["B"], c["S"], c["E"]
    QL, ESL, FSL = c["QL"], c["ESL"], c["FSL"]
    NJC, NDC, NHC = c["NJC"], c["NDC"], c["NHC"]
    x = np.asarray(I["x"], dtype=np.float32)
    cc = np.ascontiguousarray
    maps = []
    for r in range(NC):
        maps.append(dict(
            xs=cc(x[:, r * QL:(r + 1) * QL, :].reshape(B * QL, E).T),
            wupq_s=cc(I["w_up_q"][:, r * ESL:(r + 1) * ESL]),
            wups_s=cc(I["w_up_scale"][:, r * (ESL // 64):(r + 1) * (ESL // 64)]),
            wdnq_s=cc(I["w_down_q"][:, r * FSL:(r + 1) * FSL]),
            wdns_s=cc(I["w_down_scale"][:, r * (FSL // 64):(r + 1) * (FSL // 64)]),
            wqkv_s=cc(I["Wqkv"][:, r * ESL:(r + 1) * ESL]),
            wproj_s=cc(I["Wproj"][:, r * ESL:(r + 1) * ESL]),
            bqkv=cc(I["bqkv"].reshape(NJC, 128)),
            bproj=cc(I["bproj"].reshape(NDC, 128)),
            bup=cc(I["b_up"].reshape(NHC, 128)),
            bdn=cc(I["b_down"].reshape(NDC, 128)),
            aup=cc(I["w_up_lora_a"]),
            bupl=cc(I["w_up_lora_b"]),
            adn=cc(I["w_down_lora_a"]),
            bdnl=cc(I["w_down_lora_b"]),
        ))
    return maps


def assemble_out(cfg, results):
    c = derive(cfg)
    B, S, E, QL = c["B"], c["S"], c["E"], c["QL"]
    out = np.empty((B, S, E), dtype=np.float32)
    for r in range(NC):
        out[:, r * QL:(r + 1) * QL, :] = np.asarray(results[r]["y"]).T.reshape(B, QL, E)
    return out


def kernel(L0_attn_query, L0_attn_key, L0_attn_value, L0_attn_Wqkv, L0_attn_bqkv,
           L0_attn_Wproj, L0_attn_bproj, L0_ffn_w_up_q, L0_ffn_w_up_scale, L0_ffn_b_up,
           L0_ffn_w_up_lora_a, L0_ffn_w_up_lora_b, L0_ffn_w_down_q, L0_ffn_w_down_scale,
           L0_ffn_b_down, L0_ffn_w_down_lora_a, L0_ffn_w_down_lora_b):
    cfg = full_cfg()
    I = dict(
        x=np.asarray(L0_attn_query, dtype=np.float32),
        Wqkv=np.asarray(L0_attn_Wqkv, dtype=np.float32),
        bqkv=np.asarray(L0_attn_bqkv, dtype=np.float32),
        Wproj=np.asarray(L0_attn_Wproj, dtype=np.float32),
        bproj=np.asarray(L0_attn_bproj, dtype=np.float32),
        w_up_q=np.asarray(L0_ffn_w_up_q, dtype=np.int32),
        w_up_scale=np.asarray(L0_ffn_w_up_scale, dtype=np.float32),
        b_up=np.asarray(L0_ffn_b_up, dtype=np.float32),
        w_up_lora_a=np.asarray(L0_ffn_w_up_lora_a, dtype=np.float32),
        w_up_lora_b=np.asarray(L0_ffn_w_up_lora_b, dtype=np.float32),
        w_down_q=np.asarray(L0_ffn_w_down_q, dtype=np.int32),
        w_down_scale=np.asarray(L0_ffn_w_down_scale, dtype=np.float32),
        b_down=np.asarray(L0_ffn_b_down, dtype=np.float32),
        w_down_lora_a=np.asarray(L0_ffn_w_down_lora_a, dtype=np.float32),
        w_down_lora_b=np.asarray(L0_ffn_w_down_lora_b, dtype=np.float32),
    )
    nc = build(cfg)
    maps = make_in_maps(cfg, I)
    res = run_bass_kernel_spmd(nc, maps, core_ids=list(range(NC)))
    return assemble_out(cfg, res.results)


# version 27 (final)
# speedup vs baseline: 1.0280x; 1.0280x over previous
import numpy as np
from contextlib import ExitStack

import concourse.bass as bass
import concourse.mybir as mybir
from concourse.bass_utils import run_bass_kernel_spmd

F32 = mybir.dt.float32
BF16 = mybir.dt.bfloat16
I32 = mybir.dt.int32
AF = mybir.ActivationFunctionType
ALU = mybir.AluOpType
AX = mybir.AxisListType

NC = 8


def full_cfg():
    return dict(B=2, S=4096, E=2048, H=16, F=8192, R=16)


def derive(cfg):
    c = dict(cfg)
    B, S, E, H, F, R = c["B"], c["S"], c["E"], c["H"], c["F"], c["R"]
    assert E == H * 128 and F % 128 == 0 and S % NC == 0
    c["TPC"] = B * S // NC
    c["QL"] = S // NC
    c["TTOK"] = min(512, c["TPC"])
    assert c["TPC"] % c["TTOK"] == 0
    c["NTT"] = c["TPC"] // c["TTOK"]
    c["NDC"] = E // 128
    c["NHC"] = F // 128
    c["NJC"] = 3 * E // 128
    c["ESL"] = E // NC
    c["FSL"] = F // NC
    assert c["ESL"] % 128 == 0 and c["FSL"] % 128 == 0
    assert c["QL"] % 128 == 0
    c["NQT"] = c["QL"] // 128
    c["QTW"] = 128
    assert S % 1024 == 0 or S == 512
    c["KR"] = [1024] * (S // 1024) if S >= 1024 else [512]
    c["NKC"] = S // 128
    assert c["TPC"] % 512 == 0 or c["TPC"] in (128, 256)
    return c


class Sem:
    def __init__(self, stack, nc, name):
        self.h = stack.enter_context(nc.semaphore(name))
        self.n = 0

    def tok(self, k=1):
        self.n += k
        return self.n


def build(cfg):
    c = derive(cfg)
    B, S, E, H, F, R = c["B"], c["S"], c["E"], c["H"], c["F"], c["R"]
    TPC, QL, TTOK, NTT = c["TPC"], c["QL"], c["TTOK"], c["NTT"]
    NDC, NHC, NJC, ESL, FSL = c["NDC"], c["NHC"], c["NJC"], c["ESL"], c["FSL"]
    NQT, QTW, KR, NKC = c["NQT"], c["QTW"], c["KR"], c["NKC"]
    NH512 = TPC // 512 if TPC >= 512 else 1
    CW = min(512, TPC)
    NU = B * H

    nc = bass.Bass("TRN2", target_bir_lowering=False, debug=False)

    xs = nc.dram_tensor("xs", [E, TPC], F32, kind="ExternalInput").ap()
    wupq_s = nc.dram_tensor("wupq_s", [F, ESL], I32, kind="ExternalInput").ap()
    wups_s = nc.dram_tensor("wups_s", [F, ESL // 64], F32, kind="ExternalInput").ap()
    wdnq_s = nc.dram_tensor("wdnq_s", [E, FSL], I32, kind="ExternalInput").ap()
    wdns_s = nc.dram_tensor("wdns_s", [E, FSL // 64], F32, kind="ExternalInput").ap()
    wqkv_s = nc.dram_tensor("wqkv_s", [3 * E, ESL], F32, kind="ExternalInput").ap()
    wproj_s = nc.dram_tensor("wproj_s", [E, ESL], F32, kind="ExternalInput").ap()
    bqkv = nc.dram_tensor("bqkv", [NJC, 128], F32, kind="ExternalInput").ap()
    bproj = nc.dram_tensor("bproj", [NDC, 128], F32, kind="ExternalInput").ap()
    bup = nc.dram_tensor("bup", [NHC, 128], F32, kind="ExternalInput").ap()
    bdn = nc.dram_tensor("bdn", [NDC, 128], F32, kind="ExternalInput").ap()
    aup = nc.dram_tensor("aup", [E, R], F32, kind="ExternalInput").ap()
    bupl = nc.dram_tensor("bupl", [R, F], F32, kind="ExternalInput").ap()
    adn = nc.dram_tensor("adn", [F, R], F32, kind="ExternalInput").ap()
    bdnl = nc.dram_tensor("bdnl", [R, E], F32, kind="ExternalInput").ap()
    y = nc.dram_tensor("y", [E, TPC], F32, kind="ExternalOutput").ap()

    wupnat = nc.dram_tensor("wupnat", [F, ESL], BF16, kind="Internal").ap()
    wdnnat = nc.dram_tensor("wdnnat", [E, FSL], BF16, kind="Internal").ap()
    wqknat = nc.dram_tensor("wqknat", [3 * E, ESL], BF16, kind="Internal").ap()
    wprnat = nc.dram_tensor("wprnat", [E, ESL], BF16, kind="Internal").ap()

    up_sz, dn_sz, qk_sz, pr_sz = ESL * F, FSL * E, ESL * 3 * E, ESL * E
    wt_elems = up_sz + dn_sz + qk_sz + pr_sz
    wT_loc = nc.dram_tensor("wT_loc", [wt_elems // 1024, 1024], BF16, kind="Internal").ap()
    NUS = 2
    wT_gus = [nc.dram_tensor(f"wT_gu{i}", [NC * (up_sz // 1024 // NUS), 1024], BF16,
                             kind="Internal", addr_space="Shared").ap() for i in range(NUS)]
    wT_gd = nc.dram_tensor("wT_gd", [NC * (dn_sz // 1024), 1024], BF16, kind="Internal", addr_space="Shared").ap()
    wT_gr = nc.dram_tensor("wT_gr", [NC * ((qk_sz + pr_sz) // 1024), 1024], BF16, kind="Internal", addr_space="Shared").ap()
    wTgUs = [g.rearrange("(r x) cc -> r (x cc)", r=NC) for g in wT_gus]
    wTgD = wT_gd.rearrange("(r x) cc -> r (x cc)", r=NC)
    wTgR = wT_gr.rearrange("(r x) cc -> r (x cc)", r=NC)

    HG0 = max(2, H // 4)
    HG1 = H - HG0
    def kv_group(h):
        return 0 if h < HG0 else 1
    def kv_kbase(h):
        g = kv_group(h)
        hl = h - (0 if g == 0 else HG0)
        ghg = HG0 if g == 0 else HG1
        base = 0 if g == 0 else 2 * HG0 * 128 * TPC
        return base + hl * 128 * TPC, g
    def kv_vbase(h):
        g = kv_group(h)
        hl = h - (0 if g == 0 else HG0)
        ghg = HG0 if g == 0 else HG1
        base = 0 if g == 0 else 2 * HG0 * 128 * TPC
        return base + ghg * 128 * TPC + hl * TPC * 128, g
    GH0 = 2 * HG0 * 128 * TPC
    GH1 = 2 * HG1 * 128 * TPC
    kv_elems = GH0 + GH1
    kv_loc = nc.dram_tensor("kv_loc", [kv_elems // 1024, 1024], BF16, kind="Internal").ap()
    kv_g0 = nc.dram_tensor("kv_g0", [NC * (GH0 // 1024), 1024], BF16, kind="Internal", addr_space="Shared").ap()
    kv_g1 = nc.dram_tensor("kv_g1", [NC * (GH1 // 1024), 1024], BF16, kind="Internal", addr_space="Shared").ap()
    q_loc = nc.dram_tensor("q_loc", [H, 128, TPC], BF16, kind="Internal").ap()

    def wview(gsrc, off, rows_per_rank, ncols):
        v3 = gsrc[:, off:off + rows_per_rank * ncols].rearrange("r (d cc) -> r d cc", cc=ncols)
        nchunk = rows_per_rank // 128
        def block(c0, w):
            return v3[:, :, c0:c0 + w].rearrange("r (dl p) f -> p r dl f", p=128), nchunk
        return block
    upTs = [wview(g, 0, ESL, F // NUS) for g in wTgUs]
    dnT = wview(wTgD, 0, FSL, E)
    qkT = wview(wTgR, 0, ESL, 3 * E)
    prT = wview(wTgR, qk_sz, ESL, E)
    kvf = kv_loc.rearrange("a b -> (a b)")
    kvgs = [kv_g0.rearrange("(r x) cc -> r (x cc)", r=NC),
            kv_g1.rearrange("(r x) cc -> r (x cc)", r=NC)]
    wTf = wT_loc.rearrange("a b -> (a b)")

    deq_w = 4 * max(ESL, FSL)
    NPQ = 4
    at_ln = NPQ * deq_w
    tr_w = max(F, 3 * E, 2048)
    tr0 = at_ln + 5 * TPC
    scl0 = tr0 + tr_w
    CB = max(NHC * TTOK, S + 14 * QL, scl0 + 4 * 64)
    FB = max(2 * NHC * 128, NU * QL, 2 * F)
    GB = 2 * NDC * 128
    STG = max(2 * TPC, 4 * TTOK)

    with ExitStack() as st_:
        en = st_.enter_context
        x_sb = en(nc.sbuf_tensor("x_sb", [128, NDC * TPC], F32))
        xn_sb = en(nc.sbuf_tensor("xn_sb", [128, max(NDC * TPC, 2 * S + 2 * NKC * 128)], BF16))
        cbig = en(nc.sbuf_tensor("cbig", [128, CB], BF16))
        fbig = en(nc.sbuf_tensor("fbig", [128, FB], BF16))
        gbuf = en(nc.sbuf_tensor("gbuf", [128, GB], BF16))
        stage = en(nc.sbuf_tensor("stage", [128, STG], BF16))
        ones_bf = en(nc.sbuf_tensor("ones_bf", [1, 128], BF16))
        ones_col = en(nc.sbuf_tensor("ones_col", [128, 1], BF16))
        idbf = en(nc.sbuf_tensor("idbf", [128, 128], BF16))
        aup_sb = en(nc.sbuf_tensor("aup_sb", [128, NDC * R], BF16))
        adn_sb = en(nc.sbuf_tensor("adn_sb", [128, NHC * R], BF16))
        bupc = en(nc.sbuf_tensor("bupc", [128, NHC], BF16))
        bdnc = en(nc.sbuf_tensor("bdnc", [128, NDC], BF16))
        bqkc = en(nc.sbuf_tensor("bqkc", [128, NJC], BF16))
        bprc = en(nc.sbuf_tensor("bprc", [128, NDC], BF16))
        braw = en(nc.sbuf_tensor("braw", [max(NHC, NJC, NDC), 128], BF16))
        sm = en(nc.sbuf_tensor("sm", [128, 64], F32))
        PS = en(nc.psum_tensor("PS", [128, 3072], F32))
        PSB = en(nc.psum_tensor("PSB", [128, 1024], BF16))
        PSC = en(nc.psum_tensor("PSC", [128, 512], F32))

        ln_xsq = at_ln
        ln_mbc = at_ln + TPC
        ln_rbc = at_ln + 2 * TPC
        ln_rows = at_ln + 3 * TPC

        S_ = {}
        for name in ["dXL", "dW", "dG", "dLOR", "dNAT", "ns0", "ns1", "ns2", "ns3",
                     "wtA", "wtB", "dST0", "dST1",
                     "wsA", "wsB", "pq0", "pq1", "pq2", "pq3", "trA", "trB",
                     "kvA", "kvB", "kvC", "vvA", "vvB", "qA", "qB", "qC", "b32s",
                     "eT", "eV", "eS", "eG", "ccs", "dOUT", "dINIT"]:
            S_[name] = Sem(st_, nc, name)
        eT, eV, eS, eG = S_["eT"], S_["eV"], S_["eS"], S_["eG"]

        Q = {k: [] for k in "tvspg"}
        ST = {}

        def on(e, fn):
            Q[e].append(fn)

        def wait(e, sem, val):
            if val is None or val <= 0:
                return
            on(e, lambda eng, s=sem.h, v=val: eng.wait_ge(s, v))

        def dma(e, sem, out, in_):
            t = sem.tok(16)
            on(e, lambda eng, o=out, i=in_, s=sem.h: eng.dma_start(out=o, in_=i).then_inc(s, 16))
            return t

        def wblk(sem, dst2d, view, c0, nchunk):
            src4, nck = view(c0, 128)
            assert nck == nchunk
            d4 = dst2d.rearrange("p (r dl f) -> p r dl f", r=NC, dl=nchunk)
            t = None
            for dl in range(nchunk):
                t = dma("p", sem, d4[:, :, dl, :], src4[:, :, dl, :])
            return t

        def dma_tr(sem, out, in_):
            t = sem.tok(16)
            on("p", lambda eng, o=out, i=in_, s=sem.h: eng.dma_start_transpose(out=o, in_=i).then_inc(s, 16))
            return t

        def mm(out, lhsT, rhs, start, stop, inc=None):
            if inc is not None:
                t = inc.tok()
                on("t", lambda eng, o=out, l=lhsT, r=rhs, a=start, b=stop, s=inc.h:
                   eng.matmul(o, l, r, start=a, stop=b).then_inc(s, 1))
                return t
            on("t", lambda eng, o=out, l=lhsT, r=rhs, a=start, b=stop:
               eng.matmul(o, l, r, start=a, stop=b))
            return None

        def op(e, sem, meth, _lp=False, **kw):
            t = sem.tok()
            def f(eng, m=meth, kw=kw, s=sem.h, lp=_lp):
                if lp:
                    with nc.allow_low_precision(reason="deliberate bf16 intermediate"):
                        ins = getattr(eng, m)(**kw)
                else:
                    ins = getattr(eng, m)(**kw)
                ins.then_inc(s, 1)
            on(e, f)
            return t

        def tpose(out, in_, inc=None):
            np_ = in_.shape[0]
            ident = idbf[:] if np_ == 128 else idbf[0:np_, 0:np_]
            def f(eng, o=out, i=in_, d=ident, s=(inc.h if inc is not None else None)):
                with nc.allow_low_precision(reason="transpose is pure data movement"):
                    ins = eng.transpose(o, i, d)
                if s is not None:
                    ins.then_inc(s, 1)
            t = inc.tok() if inc is not None else None
            on("t", f)
            return t

        op("g", eG, "memset", ap=ones_bf[:], constant=1.0)
        op("g", eG, "memset", ap=ones_col[:], constant=1.0)
        t_idm = op("g", eG, "memset", ap=idbf[:], constant=0.0)
        wait("g", eG, t_idm)
        t_id = eG.tok()
        on("g", lambda eng: eng.affine_select(
            out=idbf[:], in_=idbf[:], compare_op=ALU.not_equal, fill=1.0,
            base=0, pattern=[[-1, 128]], channel_multiplier=1).then_inc(eG.h, 1))

        t_xl = None
        for dc in range(NDC):
            t_xl = dma("p", S_["dXL"], x_sb[:, dc * TPC:(dc + 1) * TPC],
                       xs[dc * 128:(dc + 1) * 128, :])

        t_aup = dma("g", S_["dLOR"], aup_sb[:].rearrange("p (d r) -> p d r", d=NDC),
                    aup.rearrange("(d p) r -> p d r", p=128))
        t_adn = t_aup
        for i in range(0, NHC, 16):
            nn = min(16, NHC - i)
            t_adn = dma("g", S_["dLOR"], adn_sb[:, i * R:(i + nn) * R].rearrange("p (d r) -> p d r", d=nn),
                        adn[i * 128:(i + nn) * 128, :].rearrange("(d p) r -> p d r", p=128))

        bias_specs = [(bqkv, NJC, bqkc, 1.0), (bup, NHC, bupc, 1.0),
                      (bdn, NDC, bdnc, 0.5), (bproj, NDC, bprc, 1.0)]
        t_bl = None
        for src, n, dst, scale in bias_specs:
            pass
        wait("t", S_["eG"], t_id)
        t_bias_done = None
        for bi, (src, n, dst, scale) in enumerate(bias_specs):
            if t_bias_done is not None:
                wait("g", eV, t_bias_done)
                wait("g", eT, ST[("btr", bi - 1)])
            t_bl = dma("g", S_["dINIT"], braw[0:n, :], src)
            wait("t", S_["dINIT"], t_bl)
            ST[("btr", bi)] = tpose(PSB[0:128, 0:n], braw[0:n, :], inc=eT)
            wait("v", eT, ST[("btr", bi)])
            if scale == 1.0:
                t_bias_done = op("v", eV, "tensor_copy", out=dst[:], in_=PSB[0:128, 0:n])
            else:
                t_bias_done = op("v", eV, "tensor_scalar", out=dst[:], in0=PSB[0:128, 0:n],
                                 scalar1=scale, scalar2=None, op0=ALU.mult)
        sqrt_hd = float(128.0 ** 0.5)
        wait("v", eV, t_bias_done)
        t_bias_done = op("v", eV, "tensor_scalar",
                         out=bqkc[:].rearrange("p (hh s) -> p hh s", s=3)[:, :, 0:1],
                         in0=bqkc[:].rearrange("p (hh s) -> p hh s", s=3)[:, :, 0:1],
                         scalar1=sqrt_hd, scalar2=None, op0=ALU.mult)

        pq_off = [i * deq_w for i in range(NPQ)]
        tr_off = [tr0]

        def prep_quant(qsrc, ssrc, nat, rows, cols):
            ngrp = rows // 512
            for g in range(ngrp):
                slot = g % NPQ
                sq = S_[f"pq{slot}"]
                ns = S_[f"ns{slot}"]
                prev = ST.get(("pq_deq", slot))
                if prev is not None:
                    wait("g", eV, prev)
                    wait("g", ns, ST[("pq_nat", slot)])
                tq = dma("g", sq, cbig[:, pq_off[slot]:pq_off[slot] + 4 * cols].rearrange(
                    "p (t cc) -> p t cc", t=4),
                    qsrc[g * 512:(g + 1) * 512, :].rearrange("(t p) cc -> p t cc", p=128))
                tq = dma("g", sq, cbig[:, scl0 + slot * 64:scl0 + slot * 64 + 4 * (cols // 64)].rearrange(
                    "p (t k) -> p t k", t=4),
                    ssrc[g * 512:(g + 1) * 512, :].rearrange("(t p) k -> p t k", p=128))
                wait("v", sq, tq)
                sexp = cbig[:, scl0 + slot * 64:scl0 + slot * 64 + 4 * (cols // 64)].rearrange(
                    "p (t k) -> p t k", t=4).unsqueeze(3).broadcast_to([128, 4, cols // 64, 64])
                tdq = op("v", eV, "scalar_tensor_tensor",
                         out=cbig[:, pq_off[slot]:pq_off[slot] + 4 * cols].rearrange(
                             "p (t k r) -> p t k r", t=4, k=cols // 64),
                         in0=cbig[:, pq_off[slot]:pq_off[slot] + 4 * cols].rearrange(
                             "p (t k r) -> p t k r", t=4, k=cols // 64),
                         scalar=-7.5, in1=sexp, op0=ALU.add, op1=ALU.mult)
                ST[("pq_deq", slot)] = tdq
                wait("g", eV, tdq)
                ST[("pq_nat", slot)] = dma("g", ns, nat[g * 512:(g + 1) * 512, :].rearrange(
                    "(t p) cc -> p t cc", p=128),
                    cbig[:, pq_off[slot]:pq_off[slot] + 4 * cols].rearrange("p (t cc) -> p t cc", t=4))
            return tuple(S_[f"ns{i}"].n for i in range(NPQ))

        t_nat_up = prep_quant(wupq_s, wups_s, wupnat, F, ESL)
        t_nat_dn = prep_quant(wdnq_s, wdns_s, wdnnat, E, FSL)
        ST["prep_dn_v"] = eV.n
        ST["prep_dn_ns"] = t_nat_dn
        t_cast_qk = dma("g", S_["dG"], wqknat, wqkv_s)
        t_cast_pr = dma("g", S_["dG"], wprnat, wproj_s)
        nw_up = [(S_[f"ns{i}"], t_nat_up[i]) for i in range(NPQ)]
        nw_dn = [(S_[f"ns{i}"], t_nat_dn[i]) for i in range(NPQ)]
        nw_cast = [(S_["dG"], t_cast_pr)]

        def prep_T(nat, rows, ncols, base, nat_waits):
            for i in range(ncols // 128):
                sq = S_["trA"]
                ws = S_["wtA"]
                for (nsem, ntok) in nat_waits:
                    wait("p", nsem, ntok)
                wait("p", ws, ws.n)
                ttr = dma_tr(sq, cbig[:, tr0:tr0 + rows],
                             nat[:, i * 128:(i + 1) * 128])
                wait("g", sq, ttr)
                dma("g", ws,
                    wTf[base + i * 128 * rows: base + (i + 1) * 128 * rows].rearrange(
                        "(p cc) -> p cc", p=128),
                    cbig[:, tr0:tr0 + rows])
            return None

        qF = F // NUS
        usz = up_sz // NUS
        for i in range(ESL // 128):
            sq = S_["trA"]
            ws = S_["wtA"]
            for (nsem, ntok) in nw_up:
                wait("p", nsem, ntok)
            wait("p", ws, ws.n)
            ttr = dma_tr(sq, cbig[:, tr0:tr0 + F], wupnat[:, i * 128:(i + 1) * 128])
            wait("g", sq, ttr)
            for r_ in range(NUS):
                dma("g", ws, wTf[r_ * usz + i * 128 * qF:
                                 r_ * usz + (i + 1) * 128 * qF].rearrange(
                    "(p cc) -> p cc", p=128),
                    cbig[:, tr0 + r_ * qF:tr0 + (r_ + 1) * qF])
        wait("g", S_["wtA"], S_["wtA"].n)
        t_ccwU = []
        for r_ in range(NUS):
            t_ccwU.append(S_["ccs"].tok())
            on("g", lambda eng, r2=r_: eng.collective_compute(
                "AllGather", ALU.bypass,
                ins=[wT_loc[r2 * usz // 1024:(r2 + 1) * usz // 1024, :]],
                outs=[wT_gus[r2]],
                replica_groups=[list(range(NC))]).then_inc(S_["ccs"].h, 1))
        prep_T(wdnnat, E, FSL, up_sz, nw_dn)
        wait("g", S_["wtA"], S_["wtA"].n)
        t_ccwd = S_["ccs"].tok()
        on("g", lambda eng: eng.collective_compute(
            "AllGather", ALU.bypass,
            ins=[wT_loc[up_sz // 1024:(up_sz + dn_sz) // 1024, :]], outs=[wT_gd],
            replica_groups=[list(range(NC))]).then_inc(S_["ccs"].h, 1))
        prep_T(wqknat, 3 * E, ESL, up_sz + dn_sz, nw_cast)
        prep_T(wprnat, E, ESL, up_sz + dn_sz + qk_sz, nw_cast)
        ST["prep_v_done"] = eV.n
        wait("g", S_["wtA"], S_["wtA"].n)
        t_ccw2 = S_["ccs"].tok()
        on("g", lambda eng: eng.collective_compute(
            "AllGather", ALU.bypass,
            ins=[wT_loc[(up_sz + dn_sz) // 1024:, :]], outs=[wT_gr],
            replica_groups=[list(range(NC))]).then_inc(S_["ccs"].h, 1))

        def emit_ln(tag, t_x_ready_v):
            tv_xb, t_sq, t_sqmm = {}, {}, {}
            xsq = [cbig[:, ln_xsq:ln_xsq + TPC], cbig[:, ln_mbc:ln_mbc + TPC]]
            nh = TPC // CW
            SOFF = max(TPC, 512)
            for dc in range(NDC):
                slot = dc % 2
                wait("v", eV, t_x_ready_v)
                if tag == "ln1" and dc == 0:
                    wait("v", S_["dXL"], t_xl)
                tv_xb[dc] = op("v", eV, "tensor_copy",
                               out=xn_sb[:, dc * TPC:(dc + 1) * TPC],
                               in_=x_sb[:, dc * TPC:(dc + 1) * TPC])
                wait("s", eV, tv_xb[dc])
                if dc >= 2:
                    wait("s", eT, t_sqmm[dc - 2])
                elif tag != "ln1":
                    wait("s", eT, ST.get(("lnfence", tag), 0))
                t_sq[dc] = op("s", eS, "activation", out=xsq[slot],
                              in_=xn_sb[:, dc * TPC:(dc + 1) * TPC], func=AF.Square)
                wait("t", eV, tv_xb[dc])
                if dc == 0:
                    wait("t", eV, ST.get(("vfence", tag), 0))
                    wait("t", eS, ST.get(("sfence", tag), 0))
                for h_ in range(nh):
                    mm(PS[0:1, h_ * CW:(h_ + 1) * CW], ones_col[:],
                       xn_sb[:, dc * TPC + h_ * CW:dc * TPC + (h_ + 1) * CW],
                       start=(dc == 0), stop=(dc == NDC - 1))
                wait("t", eS, t_sq[dc])
                last = None
                for h_ in range(nh):
                    last = mm(PS[0:1, SOFF + h_ * CW:SOFF + (h_ + 1) * CW], ones_col[:],
                              xsq[slot][:, h_ * CW:(h_ + 1) * CW],
                              start=(dc == 0), stop=(dc == NDC - 1),
                              inc=eT if h_ == nh - 1 else None)
                t_sqmm[dc] = last
            t_stats = t_sqmm[NDC - 1]
            nmrow = cbig[0:1, ln_rows:ln_rows + TPC]
            msrow = cbig[0:1, ln_rows + TPC:ln_rows + 2 * TPC]
            wait("v", eT, t_stats)
            t1 = op("v", eV, "tensor_scalar", out=nmrow, in0=PS[0:1, 0:TPC],
                    scalar1=-1.0 / E, scalar2=None, op0=ALU.mult)
            wait("v", eV, t1)
            t2 = op("v", eV, "tensor_mul", out=msrow, in0=nmrow, in1=nmrow)
            wait("v", eV, t2)
            t3 = op("v", eV, "scalar_tensor_tensor", out=msrow, in0=PS[0:1, SOFF:SOFF + TPC],
                    scalar=1.0 / E, in1=msrow, op0=ALU.mult, op1=ALU.subtract)
            wait("v", eV, t3)
            t4 = op("v", eV, "tensor_scalar", out=msrow, in0=msrow,
                    scalar1=1e-5, scalar2=None, op0=ALU.add)
            wait("v", eV, t4)
            t5 = op("v", eV, "reciprocal", out=msrow, in_=msrow, _lp=True)
            wait("s", eV, t5)
            t6 = op("s", eS, "activation", out=msrow, in_=msrow, func=AF.Sqrt)
            wait("t", eV, t5)
            for h_ in range(nh):
                mm(PS[:, SOFF + h_ * CW:SOFF + (h_ + 1) * CW], ones_bf[:],
                   nmrow[:, h_ * CW:(h_ + 1) * CW], start=True, stop=True)
            wait("t", eS, t6)
            t_bc = None
            for h_ in range(nh):
                t_bc = mm(PS[:, h_ * CW:(h_ + 1) * CW], ones_bf[:],
                          msrow[:, h_ * CW:(h_ + 1) * CW], start=True, stop=True,
                          inc=eT if h_ == nh - 1 else None)
            wait("v", eT, t_bc)
            op("v", eV, "tensor_copy", out=cbig[:, ln_mbc:ln_mbc + TPC], in_=PS[:, SOFF:SOFF + TPC])
            t8 = op("v", eV, "tensor_copy", out=cbig[:, ln_rbc:ln_rbc + TPC], in_=PS[:, 0:TPC])
            tv_xn = {}
            for dc in range(NDC):
                wait("v", eV, t8)
                ta = op("v", eV, "tensor_add",
                        out=xn_sb[:, dc * TPC:(dc + 1) * TPC],
                        in0=x_sb[:, dc * TPC:(dc + 1) * TPC],
                        in1=cbig[:, ln_mbc:ln_mbc + TPC])
                wait("v", eV, ta)
                tv_xn[dc] = op("v", eV, "tensor_mul",
                               out=xn_sb[:, dc * TPC:(dc + 1) * TPC],
                               in0=xn_sb[:, dc * TPC:(dc + 1) * TPC],
                               in1=cbig[:, ln_rbc:ln_rbc + TPC])
            return tv_xn

        def emit_ffn(tag, tv_xn, first):
            t_res = {}
            for tt in range(NTT):
                tok0 = tt * TTOK
                u32 = stage[0:32, 0:TTOK]
                v32 = stage[0:32, TTOK:2 * TTOK]
                wait("g", eV, eV.n)
                wait("g", eS, eS.n)
                wait("g", eT, eT.n)
                tz2 = op("g", eG, "memset", ap=stage[0:32, 0:2 * TTOK], constant=0.0)
                tz1 = op("g", eG, "memset", ap=fbig[0:32, 0:F], constant=0.0)
                wait("g", eG, tz1)
                tb32 = dma("g", S_["b32s"], fbig[0:16, 0:F], bupl)
                ubank = PSC[0:16, 0:TTOK]
                tu = None
                for dc in range(NDC):
                    wait("t", eV, tv_xn[dc])
                    if dc == 0:
                        wait("t", S_["dLOR"], t_adn)
                    tu = mm(ubank, aup_sb[:, dc * R:(dc + 1) * R],
                            xn_sb[:, dc * TPC + tok0:dc * TPC + tok0 + TTOK],
                            start=(dc == 0), stop=(dc == NDC - 1),
                            inc=eT if dc == NDC - 1 else None)
                wait("v", eT, tu)
                wait("v", eG, tz2)
                tu32 = op("v", eV, "tensor_copy", out=u32[0:16, :], in_=ubank)
                t_x2, up_last = {}, {}
                tld = {0: None, 1: None}
                UQ = NHC // NUS
                for pre in range(min(2, NHC)):
                    psl = pre % 2
                    if first and tt == 0:
                        wait("p", S_["ccs"], t_ccwU[pre // UQ])
                    wait("p", eT, eT.n)
                    tld[psl] = wblk(S_["wsA" if psl == 0 else "wsB"],
                                    gbuf[:, psl * NDC * 128:(psl + 1) * NDC * 128],
                                    upTs[pre // UQ], (pre % UQ) * 128, ESL // 128)
                t_vT = None
                for hc in range(NHC):
                    slot = hc % 2
                    wsem = S_["wsA" if slot == 0 else "wsB"]
                    wait("t", wsem, tld[slot])
                    if hc == 0:
                        wait("t", eV, tu32)
                        wait("t", S_["b32s"], tb32)
                        wait("t", eG, tz1)
                        wait("t", eV, t_bias_done)
                    pb = PS[:, slot * 512:slot * 512 + TTOK]
                    if hc >= 2:
                        wait("t", eS, t_x2[hc - 2])
                    for dc in range(NDC):
                        mm(pb, gbuf[:, slot * NDC * 128 + dc * 128:slot * NDC * 128 + (dc + 1) * 128],
                           xn_sb[:, dc * TPC + tok0:dc * TPC + tok0 + TTOK],
                           start=(dc == 0), stop=False)
                    up_last[hc] = mm(pb, fbig[0:32, hc * 128:(hc + 1) * 128], u32,
                                     start=False, stop=True, inc=eT)
                    nxt = hc + 2
                    if nxt < NHC:
                        wait("p", eT, up_last[hc])
                        if first and tt == 0 and nxt % UQ in (0, 1) and nxt >= UQ:
                            wait("p", S_["ccs"], t_ccwU[nxt // UQ])
                        tld[slot] = wblk(wsem,
                                         gbuf[:, slot * NDC * 128:(slot + 1) * NDC * 128],
                                         upTs[nxt // UQ], (nxt % UQ) * 128, ESL // 128)
                    wait("s", eT, up_last[hc])
                    if (not first) and tt == 0 and hc == 0:
                        for s2 in ("kvA", "kvB", "kvC", "vvA", "vvB",
                                   "qA", "qB", "qC"):
                            wait("s", S_[s2], S_[s2].n)
                        wait("s", eV, eV.n)
                    if first and tt == 0 and hc == 0:
                        wait("s", eV, ST["prep_dn_v"])
                        for i_ in range(NPQ):
                            wait("s", S_[f"ns{i_}"], ST["prep_dn_ns"][i_])
                    if first and tt == 0 and hc == tr0 // TTOK:
                        wait("s", S_["trA"], S_["trA"].n)
                        wait("s", S_["wtA"], S_["wtA"].n)
                    t_x2[hc] = op("s", eS, "activation",
                                  out=cbig[:, hc * TTOK:(hc + 1) * TTOK], in_=pb,
                                  func=AF.Relu, bias=bupc[:, hc:hc + 1], scale=1.0)
                    vbank = PSC[0:16, 0:TTOK]
                    wait("t", eS, t_x2[hc])
                    if hc == 0:
                        wait("t", eV, tu32)
                    t_vT = mm(vbank, adn_sb[:, hc * R:(hc + 1) * R],
                              cbig[:, hc * TTOK:(hc + 1) * TTOK],
                              start=(hc == 0), stop=(hc == NHC - 1),
                              inc=eT if hc == NHC - 1 else None)
                wait("v", eT, t_vT)
                tv32 = op("v", eV, "tensor_copy", out=v32[0:16, :], in_=vbank)
                wait("g", eT, t_vT)
                tzg = op("g", eG, "memset", ap=gbuf[0:32, 0:E], constant=0.0)
                wait("g", eG, tzg)
                tbdn = dma("g", S_["b32s"], gbuf[0:16, 0:E], bdnl)
                tldd = {0: None, 1: None}
                for pre in range(min(2, NDC)):
                    psl = pre % 2
                    wait("p", eT, t_vT)
                    if first and tt == 0:
                        wait("p", S_["ccs"], t_ccwd)
                    tldd[psl] = wblk(S_["wsA" if psl == 0 else "wsB"],
                                     fbig[:, psl * F:psl * F + F],
                                     dnT, pre * 128, FSL // 128)
                dn_last, t_rs = {}, {}
                for dc in range(NDC):
                    slot = dc % 2
                    wsem = S_["wsA" if slot == 0 else "wsB"]
                    wait("t", wsem, tldd[slot])
                    if dc == 0:
                        wait("t", eV, tv32)
                        wait("t", S_["b32s"], tbdn)
                        wait("t", eG, tzg)
                    pb = PS[:, 1024 + slot * 512:1024 + slot * 512 + TTOK]
                    if dc >= 2:
                        wait("t", eS, t_rs[dc - 2])
                    for hc in range(NHC):
                        mm(pb, fbig[:, slot * F + hc * 128:slot * F + (hc + 1) * 128],
                           cbig[:, hc * TTOK:(hc + 1) * TTOK],
                           start=(hc == 0), stop=False)
                    dn_last[dc] = mm(pb, gbuf[0:32, dc * 128:(dc + 1) * 128], v32,
                                     start=False, stop=True, inc=eT)
                    nxt = dc + 2
                    if nxt < NDC:
                        wait("p", eT, dn_last[dc])
                        tldd[slot] = wblk(wsem,
                                          fbig[:, slot * F:slot * F + F],
                                          dnT, nxt * 128, FSL // 128)
                    ystage = stage[:, TTOK * 2 + (dc % 2) * TTOK:TTOK * 2 + (dc % 2 + 1) * TTOK]
                    wait("s", eT, dn_last[dc])
                    prevy = ST.get(("ystage", dc % 2))
                    if prevy is not None:
                        wait("s", eV, prevy)
                    t_rs[dc] = op("s", eS, "activation", out=ystage, in_=pb,
                                  func=AF.Identity, bias=bdnc[:, dc:dc + 1], scale=0.5)
                    wait("v", eS, t_rs[dc])
                    t_res[dc] = op("v", eV, "tensor_add",
                                   out=x_sb[:, dc * TPC + tok0:dc * TPC + tok0 + TTOK],
                                   in0=x_sb[:, dc * TPC + tok0:dc * TPC + tok0 + TTOK],
                                   in1=ystage)
                    ST[("ystage", dc % 2)] = t_res[dc]
            return t_res

        PHASE = cfg.get("phase", "all")
        tv_xn1 = emit_ln("ln1", 0)
        if PHASE == "ln1x":
            t_out = None
            for dc in range(NDC):
                wait("g", eV, tv_xn1[dc])
                t_out = dma("g", S_["dOUT"], y[dc * 128:(dc + 1) * 128, :],
                            xn_sb[:, dc * TPC:(dc + 1) * TPC])
            wait("g", S_["dOUT"], t_out)
            return _finish(nc, en, Q)
        t_res1 = emit_ffn("ffn1", tv_xn1, first=True)
        if PHASE == "up1":
            t_out = None
            wait("g", eS, eS.n)
            for hc in range(min(NHC, NDC)):
                t_out = dma("g", S_["dOUT"], y[hc * 128:(hc + 1) * 128, 0:TTOK],
                            cbig[:, hc * TTOK:(hc + 1) * TTOK])
            wait("g", S_["dOUT"], t_out)
            return _finish(nc, en, Q)
        if PHASE == "ffn1":
            t_out = None
            for dc in range(NDC):
                wait("g", eV, t_res1[dc])
                t_out = dma("g", S_["dOUT"], y[dc * 128:(dc + 1) * 128, :],
                            x_sb[:, dc * TPC:(dc + 1) * TPC])
            wait("g", S_["dOUT"], t_out)
            return _finish(nc, en, Q)
        ST[("vfence", "ln2")] = eV.n
        ST[("sfence", "ln2")] = eS.n
        ST[("lnfence", "ln2")] = eT.n
        tv_xn2 = emit_ln("ln2", t_res1[NDC - 1])

        t_kvst, t_qst = None, None
        t_evq, qkv_last = {}, {}
        tqld = {0: None, 1: None}
        for pre in range(2):
            psl = pre % 2
            wait("p", eT, eT.n)
            wait("p", S_["ccs"], t_ccw2)
            tqld[psl] = wblk(S_["wsA" if psl == 0 else "wsB"],
                             gbuf[:, psl * NDC * 128:(psl + 1) * NDC * 128],
                             qkT, pre * 128, ESL // 128)
        for jc in range(NJC):
            slot = jc % 2
            pslot = jc % 3
            wsem = S_["wsA" if slot == 0 else "wsB"]
            wait("t", wsem, tqld[slot])
            pb = PS[:, pslot * 1024:pslot * 1024 + TPC]
            if jc >= 3:
                wait("t", eS, t_evq[jc - 3])
            elif jc == 0:
                wait("t", eV, eV.n)
                wait("t", eS, eS.n)
            for dc in range(NDC):
                if jc == 0:
                    wait("t", eV, tv_xn2[dc])
                for h_ in range(TPC // CW):
                    qkv_last[jc] = mm(pb[:, h_ * CW:(h_ + 1) * CW],
                                      gbuf[:, slot * NDC * 128 + dc * 128:slot * NDC * 128 + (dc + 1) * 128],
                                      xn_sb[:, dc * TPC + h_ * CW:dc * TPC + (h_ + 1) * CW],
                                      start=(dc == 0), stop=(dc == NDC - 1),
                                      inc=eT if (dc == NDC - 1 and h_ == TPC // CW - 1) else None)
            nxt = jc + 2
            if nxt < NJC:
                wait("p", eT, qkv_last[jc])
                tqld[slot] = wblk(wsem,
                                  gbuf[:, slot * NDC * 128:(slot + 1) * NDC * 128],
                                  qkT, nxt * 128, ESL // 128)
            h = jc // 3
            s_ = jc % 3
            sslot = jc % 2
            sls = [S_["dST0"], S_["dST1"]]
            dst = stage[:, sslot * TPC:(sslot + 1) * TPC]
            wait("s", eT, qkv_last[jc])
            wait("s", sls[sslot], sls[sslot].n)
            t_evq[jc] = op("s", eS, "activation", out=dst, in_=pb,
                           func=AF.Identity, bias=bqkc[:, jc:jc + 1],
                           scale=float(128.0 ** 0.5) if s_ == 0 else 1.0)
            if s_ == 0:
                wait("g", eS, t_evq[jc])
                t_qst = dma("g", sls[sslot], q_loc[h], dst)
            elif s_ == 1:
                wait("g", eS, t_evq[jc])
                kbase, _ = kv_kbase(h)
                t_kvst = dma("g", sls[sslot],
                             kvf[kbase:kbase + 128 * TPC].rearrange("(p cc) -> p cc", p=128),
                             dst)
            elif PHASE == "vfm":
                wait("g", eS, t_evq[jc])
                vbase, _ = kv_vbase(h)
                t_kvst = dma("g", sls[sslot],
                             kvf[vbase:vbase + TPC * 128].rearrange("(p cc) -> p cc", p=128),
                             dst)
            else:
                ntile = TPC // 128
                vps = PSB[:, 0:TPC]
                t_tr = None
                for tc in range(ntile):
                    wait("t", eS, t_evq[jc])
                    if tc == 0 and ST.get("vtr_ev") is not None:
                        wait("t", eV, ST["vtr_ev"])
                    t_tr = tpose(vps[:, tc * 128:(tc + 1) * 128], dst[:, tc * 128:(tc + 1) * 128],
                                 inc=eT if tc == ntile - 1 else None)
                oslot = 1 - sslot
                wait("v", eT, t_tr)
                wait("v", sls[oslot], sls[oslot].n)
                vdst = stage[:, oslot * TPC:(oslot + 1) * TPC]
                tev = op("v", eV, "tensor_copy", out=vdst, in_=vps)
                ST["vtr_ev"] = tev
                wait("g", eV, tev)
                vbase, _ = kv_vbase(h)
                t_kvst = dma("g", sls[oslot],
                             kvf[vbase:vbase + TPC * 128].rearrange(
                                 "(tc p f) -> p tc f", p=128, f=128),
                             vdst.rearrange("p (tc f) -> p tc f", f=128))
            if jc == 3 * HG0 - 1 and PHASE not in ("qx", "kx", "vx", "vfm"):
                wait("g", S_["dST0"], S_["dST0"].n)
                wait("g", S_["dST1"], S_["dST1"].n)
                t_cckv = S_["ccs"].tok()
                on("g", lambda eng: eng.collective_compute(
                    "AllGather", ALU.bypass, ins=[kv_loc[0:GH0 // 1024, :]], outs=[kv_g0],
                    replica_groups=[list(range(NC))]).then_inc(S_["ccs"].h, 1))

        wait("g", S_["dST0"], S_["dST0"].n)
        wait("g", S_["dST1"], S_["dST1"].n)
        if PHASE in ("qx", "kx", "vx", "vfm"):
            t_out = None
            for h_ in range(H):
                if PHASE == "qx":
                    t_out = dma("g", S_["dOUT"], y[h_ * 128:(h_ + 1) * 128, :], q_loc[h_])
                elif PHASE == "kx":
                    kb2, _ = kv_kbase(h_)
                    t_out = dma("g", S_["dOUT"], y[h_ * 128:(h_ + 1) * 128, :],
                                kvf[kb2:kb2 + 128 * TPC].rearrange(
                                    "(p cc) -> p cc", p=128))
                elif PHASE == "vfm":
                    vb2, _ = kv_vbase(h_)
                    t_out = dma("g", S_["dOUT"], y[h_ * 128:(h_ + 1) * 128, :],
                                kvf[vb2:vb2 + TPC * 128].rearrange(
                                    "(p cc) -> p cc", p=128))
                else:
                    vb2, _ = kv_vbase(h_)
                    t_out = dma("g", S_["dOUT"], y[h_ * 128:(h_ + 1) * 128, :],
                                kvf[vb2:vb2 + TPC * 128].rearrange(
                                    "(l f) -> f l", f=128))
            wait("g", S_["dOUT"], t_out)
            return _finish(nc, en, Q)
        if PHASE in ("qx", "kx", "vx", "vfm"):
            t_cckv = S_["ccs"].tok()
            on("g", lambda eng: eng.collective_compute(
                "AllGather", ALU.bypass, ins=[kv_loc[0:GH0 // 1024, :]], outs=[kv_g0],
                replica_groups=[list(range(NC))]).then_inc(S_["ccs"].h, 1))
        t_cckv2 = S_["ccs"].tok()
        on("g", lambda eng: eng.collective_compute(
            "AllGather", ALU.bypass, ins=[kv_loc[GH0 // 1024:, :]], outs=[kv_g1],
            replica_groups=[list(range(NC))]).then_inc(S_["ccs"].h, 1))

        kslot = [xn_sb[:, 0:S], xn_sb[:, S:2 * S], xn_sb[:, 2 * S:3 * S]]
        vslot = [xn_sb[:, 3 * S:4 * S], cbig[:, 0:S]]
        at0 = S
        atslot = [cbig[:, at0 + i * QL:at0 + (i + 1) * QL] for i in range(3)]
        mrow = [cbig[0:1, at0 + (3 + i) * QL:at0 + (4 + i) * QL] for i in range(2)]
        rsrow = [cbig[0:1, at0 + (5 + i) * QL:at0 + (6 + i) * QL] for i in range(2)]
        qslot = [cbig[:, at0 + (7 + i) * QL:at0 + (8 + i) * QL] for i in range(3)]
        mcol = cbig[:, at0 + 10 * QL:at0 + 10 * QL + 8]
        rsb = [cbig[:, at0 + 10 * QL + 16 + i * QL:at0 + 10 * QL + 16 + (i + 1) * QL]
               for i in range(2)]
        accsb = cbig[:, at0 + 10 * QL + 16 + 2 * QL:at0 + 10 * QL + 16 + 3 * QL]
        KHs = [HG0 * 128 * TPC, HG1 * 128 * TPC]
        kKs = [kvgs[g][:, 0:KHs[g]].rearrange("r (d l) -> d r l", l=TPC) for g in range(2)]
        kVs = [kvgs[g][:, KHs[g]:].rearrange("r (h l f) -> h r l f", l=TPC, f=128) for g in range(2)]

        useq = [(b_, h_) for h_ in range(HG0) for b_ in range(B)] + \
               [(b_, h_) for h_ in range(HG0, H) for b_ in range(B)]
        NB1 = S // 512
        NP1 = NQT * NB1
        ksem = ["kvA", "kvB", "kvC"]
        qsem = ["qA", "qB", "qC"]

        def ccwait(ui):
            g = kv_group(useq[ui][1])
            wait("p", S_["ccs"], t_cckv2 if g == 1 else t_cckv)

        def loadK(ui):
            b_, h_ = useq[ui]
            g = kv_group(h_)
            hl = h_ - (0 if g == 0 else HG0)
            ccwait(ui)
            return dma("p", S_[ksem[ui % 3]],
                       kslot[ui % 3].rearrange("p (r cc) -> p r cc", r=NC),
                       kKs[g][hl * 128:(hl + 1) * 128, :, b_ * QL:(b_ + 1) * QL])

        def loadQ(ui):
            b_, h_ = useq[ui]
            ccwait(ui)
            return dma("p", S_[qsem[ui % 3]],
                       qslot[ui % 3], q_loc[h_][:, b_ * QL:(b_ + 1) * QL])

        def loadV(ui):
            b_, h_ = useq[ui]
            g = kv_group(h_)
            hl = h_ - (0 if g == 0 else HG0)
            ccwait(ui)
            vd4 = vslot[ui % 2].rearrange("p (r sb f) -> p r sb f", r=NC, f=128)
            vs4 = kVs[g][hl, :, b_ * QL:(b_ + 1) * QL, :].rearrange(
                "r (sb p) f -> p r sb f", p=128)
            t = None
            for sb in range(QL // 128):
                t = dma("p", S_["vvA" if ui % 2 == 0 else "vvB"],
                        vd4[:, :, sb, :], vs4[:, :, sb, :])
            return t

        def uu_of(w):
            return useq[w][0] * H + useq[w][1]

        t_o_ev = {}
        kq_tok, vv_tok = {}, {}
        wait("g", eT, eT.n)
        wait("g", S_["dST0"], S_["dST0"].n)
        wait("g", S_["dST1"], S_["dST1"].n)
        tmz = op("g", eG, "memset", ap=cbig[:, at0 + 3 * QL:at0 + 5 * QL],
                 constant=0.0)
        tmz2 = op("g", eG, "memset", ap=stage[:, 0:128], constant=0.0)
        ST["mz"] = tmz
        wait("s", eG, tmz2)
        wait("s", eV, eV.n)
        op("s", eS, "copy", out=stage[0:1, 0:128], in_=ones_bf[:])
        mrowpad = [cbig[:, at0 + (3 + i) * QL:at0 + (4 + i) * QL] for i in range(2)]

        def p1_item(u_, j):
            qt, kb = j // NB1, j % NB1
            bk = j % 2
            pba = PS[:, bk * 512:bk * 512 + 512]
            prev = ST.get(("p1bank", bk))
            if prev is not None:
                wait("t", eV, prev)
            t_mm = mm(pba, qslot[u_ % 3][:, qt * QTW:(qt + 1) * QTW],
                      kslot[u_ % 3][:, kb * 512:(kb + 1) * 512],
                      start=True, stop=True, inc=eT)
            wait("v", eT, t_mm)
            t_mx = op("v", eV, "tensor_reduce", out=sm[:, j:j + 1], in_=pba,
                      axis=AX.X, op=ALU.max, negate=True)
            ST[("p1bank", bk)] = t_mx
            if kb == NB1 - 1:
                wait("v", eV, t_mx)
                if NB1 > 1:
                    t_cb = op("v", eV, "tensor_reduce", out=sm[:, 32 + qt:33 + qt],
                              in_=sm[:, qt * NB1:(qt + 1) * NB1], axis=AX.X, op=ALU.min)
                else:
                    t_cb = op("v", eV, "tensor_copy", out=sm[:, 32 + qt:33 + qt],
                              in_=sm[:, j:j + 1])
                prev_t = ST.get(("mtr", qt))
                if prev_t is not None:
                    wait("v", eT, prev_t)
                wait("v", eV, t_cb)
                ST[("mcol", qt)] = op("v", eV, "tensor_copy", _lp=True,
                                      out=mcol[:, qt:qt + 1],
                                      in_=sm[:, 32 + qt:33 + qt])

        def p1_tail(u_):
            t_tr = None
            for qt in range(NQT):
                wait("t", eV, ST[("mcol", qt)])
                if qt == 0:
                    prev_m = ST.get(("mrow", u_ - 1))
                    if prev_m is not None:
                        wait("t", eS, prev_m)
                t_tr = tpose(PSB[0:1, qt * QTW:(qt + 1) * QTW],
                             mcol[:, qt:qt + 1], inc=eT)
                ST[("mtr", qt)] = t_tr
            prev_b = ST.get(("p2mm", u_ - 2))
            if prev_b is not None:
                wait("s", eT, prev_b)
            wait("s", eT, t_tr)
            ST[("mrow", u_)] = op("s", eS, "copy", _lp=True, out=mrow[u_ % 2],
                                  in_=PSB[0:1, 0:QL])

        def p2_consume(w, kc):
            sl = kc % 3
            obw = 4 + (w % 2)
            ob_ap = PS[:, obw * 512:obw * 512 + QL]
            t_exp = ST[("pexp", w)][kc]
            wait("t", eS, t_exp)
            if kc == 0:
                wait("t", S_["vvA" if w % 2 == 0 else "vvB"], vv_tok[w])
                prev_o = ST.get(("oev", w - 1))
                if prev_o is not None:
                    wait("t", eV, prev_o)
            t_av = mm(ob_ap, vslot[w % 2][:, kc * 128:(kc + 1) * 128], atslot[sl],
                      start=(kc == 0), stop=(kc == NKC - 1), inc=eT)
            wait("v", eS, t_exp)
            if kc == 0:
                prev_pc = ST.get(("rscp", w - 1))
                if prev_pc is not None:
                    wait("v", eS, prev_pc)
                t_ac = op("v", eV, "tensor_copy", out=PSC[:, 0:QL], in_=atslot[sl])
            else:
                wait("v", eV, ST[("acc", w)])
                t_ac = op("v", eV, "tensor_add", out=PSC[:, 0:QL],
                          in0=PSC[:, 0:QL], in1=atslot[sl])
            ST[("acc", w)] = t_ac
            ST[("atrd", sl)] = (t_av, t_ac)
            if kc == NKC - 1:
                ST[("av", w)] = t_av

        def p2_item(w, kc):
            pb = 2 + (kc % 2)
            pba = PS[:, pb * 512:pb * 512 + QL]
            prev_exp = ST.get(("p2bank", pb))
            if prev_exp is not None:
                wait("t", eS, prev_exp)
            if kc == 0:
                wait("t", eS, ST[("mrow", w)])
                if w == 0:
                    wait("t", eG, ST["mz"])
                ST[("pexp", w)] = {}
            mm(pba, stage[:, 0:128], mrowpad[w % 2], start=True, stop=False)
            t_sc = mm(pba, kslot[w % 3][:, kc * 128:(kc + 1) * 128], qslot[w % 3],
                      start=False, stop=True, inc=eT)
            ST[("p2mm", w)] = t_sc
            sl = kc % 3
            wait("s", eT, t_sc)
            prev_rd = ST.get(("atrd", sl))
            if prev_rd is not None:
                wait("s", eT, prev_rd[0])
                wait("s", eV, prev_rd[1])
            t_exp = op("s", eS, "activation", _lp=True, out=atslot[sl], in_=pba,
                       func=AF.Exp, bias=0.0, scale=1.0)
            ST[("pexp", w)][kc] = t_exp
            ST[("p2bank", pb)] = t_exp
            if kc >= 1:
                p2_consume(w, kc - 1)

        def p2_tail(w):
            ww = uu_of(w)
            p2_consume(w, NKC - 1)
            wait("s", eV, ST[("acc", w)])
            prev_on = ST.get(("ones", w - 1))
            if prev_on is not None:
                wait("s", eT, prev_on)
            t_acp = op("s", eS, "copy", _lp=True, out=accsb, in_=PSC[:, 0:QL])
            wait("t", eS, t_acp)
            t_ones = mm(PSC[0:1, 0:QL], ones_col[:], accsb, start=True, stop=True,
                        inc=eT)
            ST[("ones", w)] = t_ones
            wait("v", eT, t_ones)
            prev_rs = ST.get(("rsmm", w - 2))
            if prev_rs is not None:
                wait("v", eT, prev_rs)
            t_rc = op("v", eV, "reciprocal", _lp=True, out=rsrow[w % 2],
                      in_=PSC[0:1, 0:QL])
            wait("t", eV, t_rc)
            t_rsmm = mm(PSC[:, 0:QL], ones_bf[:], rsrow[w % 2], start=True, stop=True,
                        inc=eT)
            ST[("rsmm", w)] = t_rsmm
            wait("s", eT, t_rsmm)
            prev_rb = ST.get(("rsb", w - 2))
            if prev_rb is not None:
                wait("s", eV, prev_rb)
            t_rsc = op("s", eS, "copy", _lp=True, out=rsb[w % 2], in_=PSC[:, 0:QL])
            ST[("rscp", w)] = t_rsc
            wait("v", eS, t_rsc)
            obw = 4 + (w % 2)
            t_o_ev[ww] = op("v", eV, "tensor_mul", _lp=True,
                            out=fbig[:, ww * QL:(ww + 1) * QL],
                            in0=PS[:, obw * 512:obw * 512 + QL], in1=rsb[w % 2])
            ST[("oev", w)] = t_o_ev[ww]
            ST[("rsb", w)] = t_o_ev[ww]

        NS = max(NP1, NKC)
        for u in range(NU + 1):
            if u == 0:
                wait("p", eT, eT.n)
                for pre in range(min(3, NU)):
                    kq_tok[pre] = (loadK(pre), loadQ(pre))
                for pre in range(min(2, NU)):
                    vv_tok[pre] = loadV(pre)
            if u < NU:
                ktok, qtok = kq_tok[u]
                wait("t", S_[ksem[u % 3]], ktok)
                wait("t", S_[qsem[u % 3]], qtok)
                if u == 0:
                    wait("t", eV, eV.n)
                    wait("t", eS, eS.n)
            for j in range(NS):
                if u < NU:
                    for i1 in range(j * NP1 // NS, (j + 1) * NP1 // NS):
                        p1_item(u, i1)
                if u >= 1:
                    for i2 in range(j * NKC // NS, (j + 1) * NKC // NS):
                        p2_item(u - 1, i2)
            if u < NU:
                p1_tail(u)
            if u >= 1:
                p2_tail(u - 1)
                if u + 1 < NU and u + 1 >= 2:
                    wait("p", eT, ST[("av", u - 1)])
                    vv_tok[u + 1] = loadV(u + 1)
                if u + 2 < NU:
                    wait("p", eT, ST[("p2mm", u - 1)])
                    kq_tok[u + 2] = (loadK(u + 2), loadQ(u + 2))

        if PHASE == "ox":
            t_out = None
            for u in range(NU):
                b_, h_ = u // H, u % H
                wait("g", eV, t_o_ev[u])
                t_out = dma("g", S_["dOUT"],
                            y[h_ * 128:(h_ + 1) * 128, b_ * QL:(b_ + 1) * QL],
                            fbig[:, u * QL:(u + 1) * QL])
            wait("g", S_["dOUT"], t_out)
            return _finish(nc, en, Q)
        tpld = {0: None, 1: None}
        for pre in range(min(2, NDC)):
            psl = pre % 2
            wait("p", eT, qkv_last[NJC - 1])
            tpld[psl] = wblk(S_["wsA" if psl == 0 else "wsB"],
                             gbuf[:, psl * NDC * 128:(psl + 1) * NDC * 128],
                             prT, pre * 128, ESL // 128)
        pr_last, t_evp, t_res2 = {}, {}, {}
        for dc in range(NDC):
            slot = dc % 2
            wsem = S_["wsA" if slot == 0 else "wsB"]
            wait("t", wsem, tpld[slot])
            for b_ in range(B):
                pbi = 2 * (dc % 2) + b_ if B > 1 else dc % 2
                pb = PS[:, pbi * 512:pbi * 512 + QL]
                if (dc - 2, b_) in t_evp:
                    wait("t", eS, t_evp[(dc - 2, b_)])
                elif dc == 0 and b_ == 0:
                    wait("t", eV, eV.n)
                    wait("t", eS, eS.n)
                for ec in range(NDC):
                    uu = b_ * H + ec
                    if dc == 0:
                        wait("t", eV, t_o_ev[uu])
                    pr_last[(dc, b_)] = mm(
                        pb, gbuf[:, slot * NDC * 128 + ec * 128:slot * NDC * 128 + (ec + 1) * 128],
                        fbig[:, uu * QL:(uu + 1) * QL],
                        start=(ec == 0), stop=(ec == NDC - 1), inc=eT if ec == NDC - 1 else None)
                wait("s", eT, pr_last[(dc, b_)])
                sl = pbi % 2
                zst = stage[:, sl * TPC:sl * TPC + QL]
                prevz = ST.get(("zst", sl))
                if prevz is not None:
                    wait("s", eV, prevz)
                else:
                    wait("s", S_["dST0"], S_["dST0"].n)
                    wait("s", S_["dST1"], S_["dST1"].n)
                t_evp[(dc, b_)] = op("s", eS, "activation", out=zst, in_=pb,
                                     func=AF.Identity, bias=bprc[:, dc:dc + 1], scale=1.0)
                wait("v", eS, t_evp[(dc, b_)])
                t_res2[dc] = op("v", eV, "tensor_add",
                                out=x_sb[:, dc * TPC + b_ * QL:dc * TPC + (b_ + 1) * QL],
                                in0=x_sb[:, dc * TPC + b_ * QL:dc * TPC + (b_ + 1) * QL],
                                in1=zst)
                ST[("zst", sl)] = t_res2[dc]
            nxt = dc + 2
            if nxt < NDC:
                wait("p", eT, pr_last[(dc, B - 1)])
                tpld[slot] = wblk(wsem,
                                  gbuf[:, slot * NDC * 128:(slot + 1) * NDC * 128],
                                  prT, nxt * 128, ESL // 128)

        if PHASE == "attn":
            t_out = None
            for dc in range(NDC):
                wait("g", eV, t_res2[dc])
                t_out = dma("g", S_["dOUT"], y[dc * 128:(dc + 1) * 128, :],
                            x_sb[:, dc * TPC:(dc + 1) * TPC])
            wait("g", S_["dOUT"], t_out)
            return _finish(nc, en, Q)
        ST[("vfence", "ln3")] = eV.n
        ST[("sfence", "ln3")] = eS.n
        ST[("lnfence", "ln3")] = eT.n
        tv_xn3 = emit_ln("ln3", t_res2[NDC - 1])
        t_res3 = emit_ffn("ffn2", tv_xn3, first=False)

        t_out = None
        for dc in range(NDC):
            wait("g", eV, t_res3[dc])
            t_out = dma("g", S_["dOUT"], y[dc * 128:(dc + 1) * 128, :],
                        x_sb[:, dc * TPC:(dc + 1) * TPC])
        wait("g", S_["dOUT"], t_out)

        return _finish(nc, en, Q)


def _finish(nc, en, Q):
    block = en(nc.Block())

    @block.tensor
    def _(eng):
        for f in Q["t"]:
            f(eng)

    @block.vector
    def _(eng):
        for f in Q["v"]:
            f(eng)

    @block.scalar
    def _(eng):
        for f in Q["s"]:
            f(eng)

    @block.sync
    def _(eng):
        for f in Q["p"]:
            f(eng)

    @block.gpsimd
    def _(eng):
        for f in Q["g"]:
            f(eng)

    return nc



def make_in_maps(cfg, I):
    c = derive(cfg)
    B, S, E = c["B"], c["S"], c["E"]
    QL, ESL, FSL = c["QL"], c["ESL"], c["FSL"]
    NJC, NDC, NHC = c["NJC"], c["NDC"], c["NHC"]
    x = np.asarray(I["x"], dtype=np.float32)
    cc = np.ascontiguousarray
    maps = []
    for r in range(NC):
        maps.append(dict(
            xs=cc(x[:, r * QL:(r + 1) * QL, :].reshape(B * QL, E).T),
            wupq_s=cc(I["w_up_q"][:, r * ESL:(r + 1) * ESL]),
            wups_s=cc(I["w_up_scale"][:, r * (ESL // 64):(r + 1) * (ESL // 64)]),
            wdnq_s=cc(I["w_down_q"][:, r * FSL:(r + 1) * FSL]),
            wdns_s=cc(I["w_down_scale"][:, r * (FSL // 64):(r + 1) * (FSL // 64)]),
            wqkv_s=cc(I["Wqkv"][:, r * ESL:(r + 1) * ESL]),
            wproj_s=cc(I["Wproj"][:, r * ESL:(r + 1) * ESL]),
            bqkv=cc(I["bqkv"].reshape(NJC, 128)),
            bproj=cc(I["bproj"].reshape(NDC, 128)),
            bup=cc(I["b_up"].reshape(NHC, 128)),
            bdn=cc(I["b_down"].reshape(NDC, 128)),
            aup=cc(I["w_up_lora_a"]),
            bupl=cc(I["w_up_lora_b"]),
            adn=cc(I["w_down_lora_a"]),
            bdnl=cc(I["w_down_lora_b"]),
        ))
    return maps


def assemble_out(cfg, results):
    c = derive(cfg)
    B, S, E, QL = c["B"], c["S"], c["E"], c["QL"]
    out = np.empty((B, S, E), dtype=np.float32)
    for r in range(NC):
        out[:, r * QL:(r + 1) * QL, :] = np.asarray(results[r]["y"]).T.reshape(B, QL, E)
    return out


def kernel(L0_attn_query, L0_attn_key, L0_attn_value, L0_attn_Wqkv, L0_attn_bqkv,
           L0_attn_Wproj, L0_attn_bproj, L0_ffn_w_up_q, L0_ffn_w_up_scale, L0_ffn_b_up,
           L0_ffn_w_up_lora_a, L0_ffn_w_up_lora_b, L0_ffn_w_down_q, L0_ffn_w_down_scale,
           L0_ffn_b_down, L0_ffn_w_down_lora_a, L0_ffn_w_down_lora_b):
    cfg = full_cfg()
    I = dict(
        x=np.asarray(L0_attn_query, dtype=np.float32),
        Wqkv=np.asarray(L0_attn_Wqkv, dtype=np.float32),
        bqkv=np.asarray(L0_attn_bqkv, dtype=np.float32),
        Wproj=np.asarray(L0_attn_Wproj, dtype=np.float32),
        bproj=np.asarray(L0_attn_bproj, dtype=np.float32),
        w_up_q=np.asarray(L0_ffn_w_up_q, dtype=np.int32),
        w_up_scale=np.asarray(L0_ffn_w_up_scale, dtype=np.float32),
        b_up=np.asarray(L0_ffn_b_up, dtype=np.float32),
        w_up_lora_a=np.asarray(L0_ffn_w_up_lora_a, dtype=np.float32),
        w_up_lora_b=np.asarray(L0_ffn_w_up_lora_b, dtype=np.float32),
        w_down_q=np.asarray(L0_ffn_w_down_q, dtype=np.int32),
        w_down_scale=np.asarray(L0_ffn_w_down_scale, dtype=np.float32),
        b_down=np.asarray(L0_ffn_b_down, dtype=np.float32),
        w_down_lora_a=np.asarray(L0_ffn_w_down_lora_a, dtype=np.float32),
        w_down_lora_b=np.asarray(L0_ffn_w_down_lora_b, dtype=np.float32),
    )
    nc = build(cfg)
    maps = make_in_maps(cfg, I)
    res = run_bass_kernel_spmd(nc, maps, core_ids=list(range(NC)))
    return assemble_out(cfg, res.results)


# version 28
# speedup vs baseline: 1.0422x; 1.0139x over previous
import numpy as np
from contextlib import ExitStack

import concourse.bass as bass
import concourse.mybir as mybir
from concourse.bass_utils import run_bass_kernel_spmd

F32 = mybir.dt.float32
BF16 = mybir.dt.bfloat16
I32 = mybir.dt.int32
AF = mybir.ActivationFunctionType
ALU = mybir.AluOpType
AX = mybir.AxisListType

NC = 8


def full_cfg():
    return dict(B=2, S=4096, E=2048, H=16, F=8192, R=16)


def derive(cfg):
    c = dict(cfg)
    B, S, E, H, F, R = c["B"], c["S"], c["E"], c["H"], c["F"], c["R"]
    assert E == H * 128 and F % 128 == 0 and S % NC == 0
    c["TPC"] = B * S // NC
    c["QL"] = S // NC
    c["TTOK"] = min(512, c["TPC"])
    assert c["TPC"] % c["TTOK"] == 0
    c["NTT"] = c["TPC"] // c["TTOK"]
    c["NDC"] = E // 128
    c["NHC"] = F // 128
    c["NJC"] = 3 * E // 128
    c["ESL"] = E // NC
    c["FSL"] = F // NC
    assert c["ESL"] % 128 == 0 and c["FSL"] % 128 == 0
    assert c["QL"] % 128 == 0
    c["NQT"] = c["QL"] // 128
    c["QTW"] = 128
    assert S % 1024 == 0 or S == 512
    c["KR"] = [1024] * (S // 1024) if S >= 1024 else [512]
    c["NKC"] = S // 128
    assert c["TPC"] % 512 == 0 or c["TPC"] in (128, 256)
    return c


class Sem:
    def __init__(self, stack, nc, name):
        self.h = stack.enter_context(nc.semaphore(name))
        self.n = 0

    def tok(self, k=1):
        self.n += k
        return self.n


def build(cfg):
    c = derive(cfg)
    B, S, E, H, F, R = c["B"], c["S"], c["E"], c["H"], c["F"], c["R"]
    TPC, QL, TTOK, NTT = c["TPC"], c["QL"], c["TTOK"], c["NTT"]
    NDC, NHC, NJC, ESL, FSL = c["NDC"], c["NHC"], c["NJC"], c["ESL"], c["FSL"]
    NQT, QTW, KR, NKC = c["NQT"], c["QTW"], c["KR"], c["NKC"]
    NH512 = TPC // 512 if TPC >= 512 else 1
    CW = min(512, TPC)
    NU = B * H

    nc = bass.Bass("TRN2", target_bir_lowering=False, debug=False)

    xs = nc.dram_tensor("xs", [E, TPC], F32, kind="ExternalInput").ap()
    wupq_s = nc.dram_tensor("wupq_s", [F, ESL], I32, kind="ExternalInput").ap()
    wups_s = nc.dram_tensor("wups_s", [F, ESL // 64], F32, kind="ExternalInput").ap()
    wdnq_s = nc.dram_tensor("wdnq_s", [E, FSL], I32, kind="ExternalInput").ap()
    wdns_s = nc.dram_tensor("wdns_s", [E, FSL // 64], F32, kind="ExternalInput").ap()
    wqkv_s = nc.dram_tensor("wqkv_s", [3 * E, ESL], F32, kind="ExternalInput").ap()
    wproj_s = nc.dram_tensor("wproj_s", [E, ESL], F32, kind="ExternalInput").ap()
    bqkv = nc.dram_tensor("bqkv", [NJC, 128], F32, kind="ExternalInput").ap()
    bproj = nc.dram_tensor("bproj", [NDC, 128], F32, kind="ExternalInput").ap()
    bup = nc.dram_tensor("bup", [NHC, 128], F32, kind="ExternalInput").ap()
    bdn = nc.dram_tensor("bdn", [NDC, 128], F32, kind="ExternalInput").ap()
    aup = nc.dram_tensor("aup", [E, R], F32, kind="ExternalInput").ap()
    bupl = nc.dram_tensor("bupl", [R, F], F32, kind="ExternalInput").ap()
    adn = nc.dram_tensor("adn", [F, R], F32, kind="ExternalInput").ap()
    bdnl = nc.dram_tensor("bdnl", [R, E], F32, kind="ExternalInput").ap()
    y = nc.dram_tensor("y", [E, TPC], F32, kind="ExternalOutput").ap()

    wupnat = nc.dram_tensor("wupnat", [F, ESL], BF16, kind="Internal").ap()
    wdnnat = nc.dram_tensor("wdnnat", [E, FSL], BF16, kind="Internal").ap()
    wqknat = nc.dram_tensor("wqknat", [3 * E, ESL], BF16, kind="Internal").ap()
    wprnat = nc.dram_tensor("wprnat", [E, ESL], BF16, kind="Internal").ap()

    up_sz, dn_sz, qk_sz, pr_sz = ESL * F, FSL * E, ESL * 3 * E, ESL * E
    wt_elems = up_sz + dn_sz + qk_sz + pr_sz
    wT_loc = nc.dram_tensor("wT_loc", [wt_elems // 1024, 1024], BF16, kind="Internal").ap()
    NUS = 2
    wT_gus = [nc.dram_tensor(f"wT_gu{i}", [NC * (up_sz // 1024 // NUS), 1024], BF16,
                             kind="Internal", addr_space="Shared").ap() for i in range(NUS)]
    wT_gd = nc.dram_tensor("wT_gd", [NC * (dn_sz // 1024), 1024], BF16, kind="Internal", addr_space="Shared").ap()
    wT_gr = nc.dram_tensor("wT_gr", [NC * ((qk_sz + pr_sz) // 1024), 1024], BF16, kind="Internal", addr_space="Shared").ap()
    wTgUs = [g.rearrange("(r x) cc -> r (x cc)", r=NC) for g in wT_gus]
    wTgD = wT_gd.rearrange("(r x) cc -> r (x cc)", r=NC)
    wTgR = wT_gr.rearrange("(r x) cc -> r (x cc)", r=NC)

    HG0 = max(2, H // 4)
    HG1 = H - HG0
    def kv_group(h):
        return 0 if h < HG0 else 1
    def kv_kbase(h):
        g = kv_group(h)
        hl = h - (0 if g == 0 else HG0)
        ghg = HG0 if g == 0 else HG1
        base = 0 if g == 0 else 2 * HG0 * 128 * TPC
        return base + hl * 128 * TPC, g
    def kv_vbase(h):
        g = kv_group(h)
        hl = h - (0 if g == 0 else HG0)
        ghg = HG0 if g == 0 else HG1
        base = 0 if g == 0 else 2 * HG0 * 128 * TPC
        return base + ghg * 128 * TPC + hl * TPC * 128, g
    GH0 = 2 * HG0 * 128 * TPC
    GH1 = 2 * HG1 * 128 * TPC
    kv_elems = GH0 + GH1
    kv_loc = nc.dram_tensor("kv_loc", [kv_elems // 1024, 1024], BF16, kind="Internal").ap()
    kv_g0 = nc.dram_tensor("kv_g0", [NC * (GH0 // 1024), 1024], BF16, kind="Internal", addr_space="Shared").ap()
    kv_g1 = nc.dram_tensor("kv_g1", [NC * (GH1 // 1024), 1024], BF16, kind="Internal", addr_space="Shared").ap()
    q_loc = nc.dram_tensor("q_loc", [H, 128, TPC], BF16, kind="Internal").ap()

    def wview(gsrc, off, rows_per_rank, ncols):
        v3 = gsrc[:, off:off + rows_per_rank * ncols].rearrange("r (d cc) -> r d cc", cc=ncols)
        nchunk = rows_per_rank // 128
        def block(c0, w):
            return v3[:, :, c0:c0 + w].rearrange("r (dl p) f -> p r dl f", p=128), nchunk
        return block
    upTs = [wview(g, 0, ESL, F // NUS) for g in wTgUs]
    dnT = wview(wTgD, 0, FSL, E)
    qkT = wview(wTgR, 0, ESL, 3 * E)
    prT = wview(wTgR, qk_sz, ESL, E)
    kvf = kv_loc.rearrange("a b -> (a b)")
    kvgs = [kv_g0.rearrange("(r x) cc -> r (x cc)", r=NC),
            kv_g1.rearrange("(r x) cc -> r (x cc)", r=NC)]
    wTf = wT_loc.rearrange("a b -> (a b)")

    deq_w = 4 * max(ESL, FSL)
    NPQ = 4
    at_ln = NPQ * deq_w
    tr_w = max(F, 3 * E, 2048)
    tr0 = at_ln + 5 * TPC
    scl0 = tr0 + tr_w
    CB = max(NHC * TTOK, S + 14 * QL, scl0 + 4 * 64)
    FB = max(2 * NHC * 128, NU * QL, 2 * F)
    GB = 2 * NDC * 128
    STG = max(2 * TPC, 4 * TTOK)

    with ExitStack() as st_:
        en = st_.enter_context
        x_sb = en(nc.sbuf_tensor("x_sb", [128, NDC * TPC], F32))
        xn_sb = en(nc.sbuf_tensor("xn_sb", [128, max(NDC * TPC, 2 * S + 2 * NKC * 128)], BF16))
        cbig = en(nc.sbuf_tensor("cbig", [128, CB], BF16))
        fbig = en(nc.sbuf_tensor("fbig", [128, FB], BF16))
        gbuf = en(nc.sbuf_tensor("gbuf", [128, GB], BF16))
        stage = en(nc.sbuf_tensor("stage", [128, STG], BF16))
        ones_bf = en(nc.sbuf_tensor("ones_bf", [1, 128], BF16))
        ones_col = en(nc.sbuf_tensor("ones_col", [128, 1], BF16))
        idbf = en(nc.sbuf_tensor("idbf", [128, 128], BF16))
        aup_sb = en(nc.sbuf_tensor("aup_sb", [128, NDC * R], BF16))
        adn_sb = en(nc.sbuf_tensor("adn_sb", [128, NHC * R], BF16))
        bupc = en(nc.sbuf_tensor("bupc", [128, NHC], BF16))
        bdnc = en(nc.sbuf_tensor("bdnc", [128, NDC], BF16))
        bqkc = en(nc.sbuf_tensor("bqkc", [128, NJC], BF16))
        bprc = en(nc.sbuf_tensor("bprc", [128, NDC], BF16))
        braw = en(nc.sbuf_tensor("braw", [max(NHC, NJC, NDC), 128], BF16))
        sm = en(nc.sbuf_tensor("sm", [128, 64], F32))
        PS = en(nc.psum_tensor("PS", [128, 3072], F32))
        PSB = en(nc.psum_tensor("PSB", [128, 1024], BF16))
        PSC = en(nc.psum_tensor("PSC", [128, 512], F32))

        ln_xsq = at_ln
        ln_mbc = at_ln + TPC
        ln_rbc = at_ln + 2 * TPC
        ln_rows = at_ln + 3 * TPC

        S_ = {}
        for name in ["dXL", "dW", "dG", "dLOR", "dNAT", "ns0", "ns1", "ns2", "ns3",
                     "wtA", "wtB", "dST0", "dST1",
                     "wsA", "wsB", "pq0", "pq1", "pq2", "pq3", "trA", "trB",
                     "kvA", "kvB", "kvC", "vvA", "vvB", "qA", "qB", "qC", "b32s",
                     "eT", "eV", "eS", "eG", "ccs", "dOUT", "dINIT"]:
            S_[name] = Sem(st_, nc, name)
        eT, eV, eS, eG = S_["eT"], S_["eV"], S_["eS"], S_["eG"]

        Q = {k: [] for k in "tvspg"}
        ST = {}

        def on(e, fn):
            Q[e].append(fn)

        def wait(e, sem, val):
            if val is None or val <= 0:
                return
            on(e, lambda eng, s=sem.h, v=val: eng.wait_ge(s, v))

        def dma(e, sem, out, in_):
            t = sem.tok(16)
            on(e, lambda eng, o=out, i=in_, s=sem.h: eng.dma_start(out=o, in_=i).then_inc(s, 16))
            return t

        def wblk(sem, dst2d, view, c0, nchunk):
            src4, nck = view(c0, 128)
            assert nck == nchunk
            d4 = dst2d.rearrange("p (r dl f) -> p r dl f", r=NC, dl=nchunk)
            t = None
            for dl in range(nchunk):
                t = dma("p", sem, d4[:, :, dl, :], src4[:, :, dl, :])
            return t

        def dma_tr(sem, out, in_):
            t = sem.tok(16)
            on("p", lambda eng, o=out, i=in_, s=sem.h: eng.dma_start_transpose(out=o, in_=i).then_inc(s, 16))
            return t

        def mm(out, lhsT, rhs, start, stop, inc=None):
            if inc is not None:
                t = inc.tok()
                on("t", lambda eng, o=out, l=lhsT, r=rhs, a=start, b=stop, s=inc.h:
                   eng.matmul(o, l, r, start=a, stop=b).then_inc(s, 1))
                return t
            on("t", lambda eng, o=out, l=lhsT, r=rhs, a=start, b=stop:
               eng.matmul(o, l, r, start=a, stop=b))
            return None

        def op(e, sem, meth, _lp=False, **kw):
            t = sem.tok()
            def f(eng, m=meth, kw=kw, s=sem.h, lp=_lp):
                if lp:
                    with nc.allow_low_precision(reason="deliberate bf16 intermediate"):
                        ins = getattr(eng, m)(**kw)
                else:
                    ins = getattr(eng, m)(**kw)
                ins.then_inc(s, 1)
            on(e, f)
            return t

        def tpose(out, in_, inc=None):
            np_ = in_.shape[0]
            ident = idbf[:] if np_ == 128 else idbf[0:np_, 0:np_]
            def f(eng, o=out, i=in_, d=ident, s=(inc.h if inc is not None else None)):
                with nc.allow_low_precision(reason="transpose is pure data movement"):
                    ins = eng.transpose(o, i, d)
                if s is not None:
                    ins.then_inc(s, 1)
            t = inc.tok() if inc is not None else None
            on("t", f)
            return t

        op("g", eG, "memset", ap=ones_bf[:], constant=1.0)
        op("g", eG, "memset", ap=ones_col[:], constant=1.0)
        t_idm = op("g", eG, "memset", ap=idbf[:], constant=0.0)
        wait("g", eG, t_idm)
        t_id = eG.tok()
        on("g", lambda eng: eng.affine_select(
            out=idbf[:], in_=idbf[:], compare_op=ALU.not_equal, fill=1.0,
            base=0, pattern=[[-1, 128]], channel_multiplier=1).then_inc(eG.h, 1))

        t_xl = None
        for dc in range(NDC):
            t_xl = dma("p", S_["dXL"], x_sb[:, dc * TPC:(dc + 1) * TPC],
                       xs[dc * 128:(dc + 1) * 128, :])

        t_aup = dma("g", S_["dLOR"], aup_sb[:].rearrange("p (d r) -> p d r", d=NDC),
                    aup.rearrange("(d p) r -> p d r", p=128))
        t_adn = t_aup
        for i in range(0, NHC, 16):
            nn = min(16, NHC - i)
            t_adn = dma("g", S_["dLOR"], adn_sb[:, i * R:(i + nn) * R].rearrange("p (d r) -> p d r", d=nn),
                        adn[i * 128:(i + nn) * 128, :].rearrange("(d p) r -> p d r", p=128))

        bias_specs = [(bqkv, NJC, bqkc, 1.0), (bup, NHC, bupc, 1.0),
                      (bdn, NDC, bdnc, 0.5), (bproj, NDC, bprc, 1.0)]
        t_bl = None
        for src, n, dst, scale in bias_specs:
            pass
        wait("t", S_["eG"], t_id)
        t_bias_done = None
        for bi, (src, n, dst, scale) in enumerate(bias_specs):
            if t_bias_done is not None:
                wait("g", eV, t_bias_done)
                wait("g", eT, ST[("btr", bi - 1)])
            t_bl = dma("g", S_["dINIT"], braw[0:n, :], src)
            wait("t", S_["dINIT"], t_bl)
            ST[("btr", bi)] = tpose(PSB[0:128, 0:n], braw[0:n, :], inc=eT)
            wait("v", eT, ST[("btr", bi)])
            if scale == 1.0:
                t_bias_done = op("v", eV, "tensor_copy", out=dst[:], in_=PSB[0:128, 0:n])
            else:
                t_bias_done = op("v", eV, "tensor_scalar", out=dst[:], in0=PSB[0:128, 0:n],
                                 scalar1=scale, scalar2=None, op0=ALU.mult)
        sqrt_hd = float(128.0 ** 0.5)
        wait("v", eV, t_bias_done)
        t_bias_done = op("v", eV, "tensor_scalar",
                         out=bqkc[:].rearrange("p (hh s) -> p hh s", s=3)[:, :, 0:1],
                         in0=bqkc[:].rearrange("p (hh s) -> p hh s", s=3)[:, :, 0:1],
                         scalar1=sqrt_hd, scalar2=None, op0=ALU.mult)

        pq_off = [i * deq_w for i in range(NPQ)]
        tr_off = [tr0]

        def prep_quant(qsrc, ssrc, nat, rows, cols):
            ngrp = rows // 512
            for g in range(ngrp):
                slot = g % NPQ
                sq = S_[f"pq{slot}"]
                ns = S_[f"ns{slot}"]
                prev = ST.get(("pq_deq", slot))
                if prev is not None:
                    wait("g", eV, prev)
                    wait("g", ns, ST[("pq_nat", slot)])
                tq = dma("g", sq, cbig[:, pq_off[slot]:pq_off[slot] + 4 * cols].rearrange(
                    "p (t cc) -> p t cc", t=4),
                    qsrc[g * 512:(g + 1) * 512, :].rearrange("(t p) cc -> p t cc", p=128))
                tq = dma("g", sq, cbig[:, scl0 + slot * 64:scl0 + slot * 64 + 4 * (cols // 64)].rearrange(
                    "p (t k) -> p t k", t=4),
                    ssrc[g * 512:(g + 1) * 512, :].rearrange("(t p) k -> p t k", p=128))
                wait("v", sq, tq)
                sexp = cbig[:, scl0 + slot * 64:scl0 + slot * 64 + 4 * (cols // 64)].rearrange(
                    "p (t k) -> p t k", t=4).unsqueeze(3).broadcast_to([128, 4, cols // 64, 64])
                tdq = op("v", eV, "scalar_tensor_tensor",
                         out=cbig[:, pq_off[slot]:pq_off[slot] + 4 * cols].rearrange(
                             "p (t k r) -> p t k r", t=4, k=cols // 64),
                         in0=cbig[:, pq_off[slot]:pq_off[slot] + 4 * cols].rearrange(
                             "p (t k r) -> p t k r", t=4, k=cols // 64),
                         scalar=-7.5, in1=sexp, op0=ALU.add, op1=ALU.mult)
                ST[("pq_deq", slot)] = tdq
                wait("g", eV, tdq)
                ST[("pq_nat", slot)] = dma("g", ns, nat[g * 512:(g + 1) * 512, :].rearrange(
                    "(t p) cc -> p t cc", p=128),
                    cbig[:, pq_off[slot]:pq_off[slot] + 4 * cols].rearrange("p (t cc) -> p t cc", t=4))
            return tuple(S_[f"ns{i}"].n for i in range(NPQ))

        t_nat_up = prep_quant(wupq_s, wups_s, wupnat, F, ESL)
        t_nat_dn = prep_quant(wdnq_s, wdns_s, wdnnat, E, FSL)
        ST["prep_dn_v"] = eV.n
        ST["prep_dn_ns"] = t_nat_dn
        t_cast_qk = dma("g", S_["dG"], wqknat, wqkv_s)
        t_cast_pr = dma("g", S_["dG"], wprnat, wproj_s)
        nw_up = [(S_[f"ns{i}"], t_nat_up[i]) for i in range(NPQ)]
        nw_dn = [(S_[f"ns{i}"], t_nat_dn[i]) for i in range(NPQ)]
        nw_cast = [(S_["dG"], t_cast_pr)]

        def prep_T(nat, rows, ncols, base, nat_waits):
            for i in range(ncols // 128):
                sq = S_["trA"]
                ws = S_["wtA"]
                for (nsem, ntok) in nat_waits:
                    wait("p", nsem, ntok)
                wait("p", ws, ws.n)
                ttr = dma_tr(sq, cbig[:, tr0:tr0 + rows],
                             nat[:, i * 128:(i + 1) * 128])
                wait("g", sq, ttr)
                dma("g", ws,
                    wTf[base + i * 128 * rows: base + (i + 1) * 128 * rows].rearrange(
                        "(p cc) -> p cc", p=128),
                    cbig[:, tr0:tr0 + rows])
            return None

        qF = F // NUS
        usz = up_sz // NUS
        for i in range(ESL // 128):
            sq = S_["trA"]
            ws = S_["wtA"]
            for (nsem, ntok) in nw_up:
                wait("p", nsem, ntok)
            wait("p", ws, ws.n)
            ttr = dma_tr(sq, cbig[:, tr0:tr0 + F], wupnat[:, i * 128:(i + 1) * 128])
            wait("g", sq, ttr)
            for r_ in range(NUS):
                dma("g", ws, wTf[r_ * usz + i * 128 * qF:
                                 r_ * usz + (i + 1) * 128 * qF].rearrange(
                    "(p cc) -> p cc", p=128),
                    cbig[:, tr0 + r_ * qF:tr0 + (r_ + 1) * qF])
        wait("g", S_["wtA"], S_["wtA"].n)
        t_ccwU = []
        for r_ in range(NUS):
            t_ccwU.append(S_["ccs"].tok())
            on("g", lambda eng, r2=r_: eng.collective_compute(
                "AllGather", ALU.bypass,
                ins=[wT_loc[r2 * usz // 1024:(r2 + 1) * usz // 1024, :]],
                outs=[wT_gus[r2]],
                replica_groups=[list(range(NC))]).then_inc(S_["ccs"].h, 1))
        prep_T(wdnnat, E, FSL, up_sz, nw_dn)
        wait("g", S_["wtA"], S_["wtA"].n)
        t_ccwd = S_["ccs"].tok()
        on("g", lambda eng: eng.collective_compute(
            "AllGather", ALU.bypass,
            ins=[wT_loc[up_sz // 1024:(up_sz + dn_sz) // 1024, :]], outs=[wT_gd],
            replica_groups=[list(range(NC))]).then_inc(S_["ccs"].h, 1))
        prep_T(wqknat, 3 * E, ESL, up_sz + dn_sz, nw_cast)
        prep_T(wprnat, E, ESL, up_sz + dn_sz + qk_sz, nw_cast)
        ST["prep_v_done"] = eV.n
        wait("g", S_["wtA"], S_["wtA"].n)
        t_ccw2 = S_["ccs"].tok()
        on("g", lambda eng: eng.collective_compute(
            "AllGather", ALU.bypass,
            ins=[wT_loc[(up_sz + dn_sz) // 1024:, :]], outs=[wT_gr],
            replica_groups=[list(range(NC))]).then_inc(S_["ccs"].h, 1))

        def emit_ln(tag, t_x_ready_v):
            tv_xb, t_sq, t_sqmm = {}, {}, {}
            xsq = [cbig[:, ln_xsq:ln_xsq + TPC], cbig[:, ln_mbc:ln_mbc + TPC]]
            nh = TPC // CW
            SOFF = max(TPC, 512)
            for dc in range(NDC):
                slot = dc % 2
                wait("v", eV, t_x_ready_v)
                if tag == "ln1" and dc == 0:
                    wait("v", S_["dXL"], t_xl)
                tv_xb[dc] = op("v", eV, "tensor_copy",
                               out=xn_sb[:, dc * TPC:(dc + 1) * TPC],
                               in_=x_sb[:, dc * TPC:(dc + 1) * TPC])
                wait("s", eV, tv_xb[dc])
                if dc >= 2:
                    wait("s", eT, t_sqmm[dc - 2])
                elif tag != "ln1":
                    wait("s", eT, ST.get(("lnfence", tag), 0))
                t_sq[dc] = op("s", eS, "activation", out=xsq[slot],
                              in_=xn_sb[:, dc * TPC:(dc + 1) * TPC], func=AF.Square)
                wait("t", eV, tv_xb[dc])
                if dc == 0:
                    wait("t", eV, ST.get(("vfence", tag), 0))
                    wait("t", eS, ST.get(("sfence", tag), 0))
                for h_ in range(nh):
                    mm(PS[0:1, h_ * CW:(h_ + 1) * CW], ones_col[:],
                       xn_sb[:, dc * TPC + h_ * CW:dc * TPC + (h_ + 1) * CW],
                       start=(dc == 0), stop=(dc == NDC - 1))
                wait("t", eS, t_sq[dc])
                last = None
                for h_ in range(nh):
                    last = mm(PS[0:1, SOFF + h_ * CW:SOFF + (h_ + 1) * CW], ones_col[:],
                              xsq[slot][:, h_ * CW:(h_ + 1) * CW],
                              start=(dc == 0), stop=(dc == NDC - 1),
                              inc=eT if h_ == nh - 1 else None)
                t_sqmm[dc] = last
            t_stats = t_sqmm[NDC - 1]
            nmrow = cbig[0:1, ln_rows:ln_rows + TPC]
            msrow = cbig[0:1, ln_rows + TPC:ln_rows + 2 * TPC]
            wait("v", eT, t_stats)
            t1 = op("v", eV, "tensor_scalar", out=nmrow, in0=PS[0:1, 0:TPC],
                    scalar1=-1.0 / E, scalar2=None, op0=ALU.mult)
            wait("v", eV, t1)
            t2 = op("v", eV, "tensor_mul", out=msrow, in0=nmrow, in1=nmrow)
            wait("v", eV, t2)
            t3 = op("v", eV, "scalar_tensor_tensor", out=msrow, in0=PS[0:1, SOFF:SOFF + TPC],
                    scalar=1.0 / E, in1=msrow, op0=ALU.mult, op1=ALU.subtract)
            wait("v", eV, t3)
            t4 = op("v", eV, "tensor_scalar", out=msrow, in0=msrow,
                    scalar1=1e-5, scalar2=None, op0=ALU.add)
            wait("v", eV, t4)
            t5 = op("v", eV, "reciprocal", out=msrow, in_=msrow, _lp=True)
            wait("s", eV, t5)
            t6 = op("s", eS, "activation", out=msrow, in_=msrow, func=AF.Sqrt)
            wait("t", eV, t5)
            for h_ in range(nh):
                mm(PS[:, SOFF + h_ * CW:SOFF + (h_ + 1) * CW], ones_bf[:],
                   nmrow[:, h_ * CW:(h_ + 1) * CW], start=True, stop=True)
            wait("t", eS, t6)
            t_bc = None
            for h_ in range(nh):
                t_bc = mm(PS[:, h_ * CW:(h_ + 1) * CW], ones_bf[:],
                          msrow[:, h_ * CW:(h_ + 1) * CW], start=True, stop=True,
                          inc=eT if h_ == nh - 1 else None)
            wait("v", eT, t_bc)
            op("v", eV, "tensor_copy", out=cbig[:, ln_mbc:ln_mbc + TPC], in_=PS[:, SOFF:SOFF + TPC])
            t8 = op("v", eV, "tensor_copy", out=cbig[:, ln_rbc:ln_rbc + TPC], in_=PS[:, 0:TPC])
            tv_xn = {}
            for dc in range(NDC):
                wait("v", eV, t8)
                ta = op("v", eV, "tensor_add",
                        out=xn_sb[:, dc * TPC:(dc + 1) * TPC],
                        in0=x_sb[:, dc * TPC:(dc + 1) * TPC],
                        in1=cbig[:, ln_mbc:ln_mbc + TPC])
                wait("v", eV, ta)
                tv_xn[dc] = op("v", eV, "tensor_mul",
                               out=xn_sb[:, dc * TPC:(dc + 1) * TPC],
                               in0=xn_sb[:, dc * TPC:(dc + 1) * TPC],
                               in1=cbig[:, ln_rbc:ln_rbc + TPC])
            return tv_xn

        def emit_ffn(tag, tv_xn, first):
            t_res = {}
            for tt in range(NTT):
                tok0 = tt * TTOK
                u32 = stage[0:32, 0:TTOK]
                v32 = stage[0:32, TTOK:2 * TTOK]
                wait("g", eV, eV.n)
                wait("g", eS, eS.n)
                wait("g", eT, eT.n)
                tz2 = op("g", eG, "memset", ap=stage[0:32, 0:2 * TTOK], constant=0.0)
                tz1 = op("g", eG, "memset", ap=fbig[0:32, 0:F], constant=0.0)
                wait("g", eG, tz1)
                tb32 = dma("g", S_["b32s"], fbig[0:16, 0:F], bupl)
                ubank = PSC[0:16, 0:TTOK]
                tu = None
                for dc in range(NDC):
                    wait("t", eV, tv_xn[dc])
                    if dc == 0:
                        wait("t", S_["dLOR"], t_adn)
                    tu = mm(ubank, aup_sb[:, dc * R:(dc + 1) * R],
                            xn_sb[:, dc * TPC + tok0:dc * TPC + tok0 + TTOK],
                            start=(dc == 0), stop=(dc == NDC - 1),
                            inc=eT if dc == NDC - 1 else None)
                wait("v", eT, tu)
                wait("v", eG, tz2)
                tu32 = op("v", eV, "tensor_copy", out=u32[0:16, :], in_=ubank)
                t_x2, up_last = {}, {}
                tld = {0: None, 1: None}
                UQ = NHC // NUS
                for pre in range(min(2, NHC)):
                    psl = pre % 2
                    if first and tt == 0:
                        wait("p", S_["ccs"], t_ccwU[pre // UQ])
                    wait("p", eT, eT.n)
                    tld[psl] = wblk(S_["wsA" if psl == 0 else "wsB"],
                                    gbuf[:, psl * NDC * 128:(psl + 1) * NDC * 128],
                                    upTs[pre // UQ], (pre % UQ) * 128, ESL // 128)
                t_vT = None
                UPB = [0, 512, 2048]
                for hc in range(NHC):
                    slot = hc % 2
                    wsem = S_["wsA" if slot == 0 else "wsB"]
                    wait("t", wsem, tld[slot])
                    if hc == 0:
                        wait("t", eV, tu32)
                        wait("t", S_["b32s"], tb32)
                        wait("t", eG, tz1)
                        wait("t", eV, t_bias_done)
                    pb = PS[:, UPB[hc % 3]:UPB[hc % 3] + TTOK]
                    if hc >= 3:
                        wait("t", eS, t_x2[hc - 3])
                    for dc in range(NDC):
                        mm(pb, gbuf[:, slot * NDC * 128 + dc * 128:slot * NDC * 128 + (dc + 1) * 128],
                           xn_sb[:, dc * TPC + tok0:dc * TPC + tok0 + TTOK],
                           start=(dc == 0), stop=False)
                    up_last[hc] = mm(pb, fbig[0:32, hc * 128:(hc + 1) * 128], u32,
                                     start=False, stop=True, inc=eT)
                    nxt = hc + 2
                    if nxt < NHC:
                        wait("p", eT, up_last[hc])
                        if first and tt == 0 and nxt % UQ in (0, 1) and nxt >= UQ:
                            wait("p", S_["ccs"], t_ccwU[nxt // UQ])
                        tld[slot] = wblk(wsem,
                                         gbuf[:, slot * NDC * 128:(slot + 1) * NDC * 128],
                                         upTs[nxt // UQ], (nxt % UQ) * 128, ESL // 128)
                    wait("s", eT, up_last[hc])
                    if (not first) and tt == 0 and hc == 0:
                        for s2 in ("kvA", "kvB", "kvC", "vvA", "vvB",
                                   "qA", "qB", "qC"):
                            wait("s", S_[s2], S_[s2].n)
                        wait("s", eV, eV.n)
                    if first and tt == 0 and hc == 0:
                        wait("s", eV, ST["prep_dn_v"])
                        for i_ in range(NPQ):
                            wait("s", S_[f"ns{i_}"], ST["prep_dn_ns"][i_])
                    if first and tt == 0 and hc == tr0 // TTOK:
                        wait("s", S_["trA"], S_["trA"].n)
                        wait("s", S_["wtA"], S_["wtA"].n)
                    t_x2[hc] = op("s", eS, "activation",
                                  out=cbig[:, hc * TTOK:(hc + 1) * TTOK], in_=pb,
                                  func=AF.Relu, bias=bupc[:, hc:hc + 1], scale=1.0)
                    vbank = PSC[0:16, 0:TTOK]
                    wait("t", eS, t_x2[hc])
                    if hc == 0:
                        wait("t", eV, tu32)
                    t_vT = mm(vbank, adn_sb[:, hc * R:(hc + 1) * R],
                              cbig[:, hc * TTOK:(hc + 1) * TTOK],
                              start=(hc == 0), stop=(hc == NHC - 1),
                              inc=eT if hc == NHC - 1 else None)
                wait("v", eT, t_vT)
                tv32 = op("v", eV, "tensor_copy", out=v32[0:16, :], in_=vbank)
                wait("g", eT, t_vT)
                tzg = op("g", eG, "memset", ap=gbuf[0:32, 0:E], constant=0.0)
                wait("g", eG, tzg)
                tbdn = dma("g", S_["b32s"], gbuf[0:16, 0:E], bdnl)
                tldd = {0: None, 1: None}
                for pre in range(min(2, NDC)):
                    psl = pre % 2
                    wait("p", eT, t_vT)
                    if first and tt == 0:
                        wait("p", S_["ccs"], t_ccwd)
                    tldd[psl] = wblk(S_["wsA" if psl == 0 else "wsB"],
                                     fbig[:, psl * F:psl * F + F],
                                     dnT, pre * 128, FSL // 128)
                dn_last, t_rs = {}, {}
                for dc in range(NDC):
                    slot = dc % 2
                    wsem = S_["wsA" if slot == 0 else "wsB"]
                    wait("t", wsem, tldd[slot])
                    if dc == 0:
                        wait("t", eV, tv32)
                        wait("t", S_["b32s"], tbdn)
                        wait("t", eG, tzg)
                    DNB = [1024, 1536, 2560]
                    pb = PS[:, DNB[dc % 3]:DNB[dc % 3] + TTOK]
                    if dc >= 3:
                        wait("t", eS, t_rs[dc - 3])
                    for hc in range(NHC):
                        mm(pb, fbig[:, slot * F + hc * 128:slot * F + (hc + 1) * 128],
                           cbig[:, hc * TTOK:(hc + 1) * TTOK],
                           start=(hc == 0), stop=False)
                    dn_last[dc] = mm(pb, gbuf[0:32, dc * 128:(dc + 1) * 128], v32,
                                     start=False, stop=True, inc=eT)
                    nxt = dc + 2
                    if nxt < NDC:
                        wait("p", eT, dn_last[dc])
                        tldd[slot] = wblk(wsem,
                                          fbig[:, slot * F:slot * F + F],
                                          dnT, nxt * 128, FSL // 128)
                    ystage = stage[:, TTOK * 2 + (dc % 2) * TTOK:TTOK * 2 + (dc % 2 + 1) * TTOK]
                    wait("s", eT, dn_last[dc])
                    prevy = ST.get(("ystage", dc % 2))
                    if prevy is not None:
                        wait("s", eV, prevy)
                    t_rs[dc] = op("s", eS, "activation", out=ystage, in_=pb,
                                  func=AF.Identity, bias=bdnc[:, dc:dc + 1], scale=0.5)
                    wait("v", eS, t_rs[dc])
                    t_res[dc] = op("v", eV, "tensor_add",
                                   out=x_sb[:, dc * TPC + tok0:dc * TPC + tok0 + TTOK],
                                   in0=x_sb[:, dc * TPC + tok0:dc * TPC + tok0 + TTOK],
                                   in1=ystage)
                    ST[("ystage", dc % 2)] = t_res[dc]
            return t_res

        PHASE = cfg.get("phase", "all")
        tv_xn1 = emit_ln("ln1", 0)
        if PHASE == "ln1x":
            t_out = None
            for dc in range(NDC):
                wait("g", eV, tv_xn1[dc])
                t_out = dma("g", S_["dOUT"], y[dc * 128:(dc + 1) * 128, :],
                            xn_sb[:, dc * TPC:(dc + 1) * TPC])
            wait("g", S_["dOUT"], t_out)
            return _finish(nc, en, Q)
        t_res1 = emit_ffn("ffn1", tv_xn1, first=True)
        if PHASE == "up1":
            t_out = None
            wait("g", eS, eS.n)
            for hc in range(min(NHC, NDC)):
                t_out = dma("g", S_["dOUT"], y[hc * 128:(hc + 1) * 128, 0:TTOK],
                            cbig[:, hc * TTOK:(hc + 1) * TTOK])
            wait("g", S_["dOUT"], t_out)
            return _finish(nc, en, Q)
        if PHASE == "ffn1":
            t_out = None
            for dc in range(NDC):
                wait("g", eV, t_res1[dc])
                t_out = dma("g", S_["dOUT"], y[dc * 128:(dc + 1) * 128, :],
                            x_sb[:, dc * TPC:(dc + 1) * TPC])
            wait("g", S_["dOUT"], t_out)
            return _finish(nc, en, Q)
        ST[("vfence", "ln2")] = eV.n
        ST[("sfence", "ln2")] = eS.n
        ST[("lnfence", "ln2")] = eT.n
        tv_xn2 = emit_ln("ln2", t_res1[NDC - 1])

        t_kvst, t_qst = None, None
        t_evq, qkv_last = {}, {}
        tqld = {0: None, 1: None}
        for pre in range(2):
            psl = pre % 2
            wait("p", eT, eT.n)
            wait("p", S_["ccs"], t_ccw2)
            tqld[psl] = wblk(S_["wsA" if psl == 0 else "wsB"],
                             gbuf[:, psl * NDC * 128:(psl + 1) * NDC * 128],
                             qkT, pre * 128, ESL // 128)
        for jc in range(NJC):
            slot = jc % 2
            pslot = jc % 3
            wsem = S_["wsA" if slot == 0 else "wsB"]
            wait("t", wsem, tqld[slot])
            pb = PS[:, pslot * 1024:pslot * 1024 + TPC]
            if jc >= 3:
                wait("t", eS, t_evq[jc - 3])
            elif jc == 0:
                wait("t", eV, eV.n)
                wait("t", eS, eS.n)
            for dc in range(NDC):
                if jc == 0:
                    wait("t", eV, tv_xn2[dc])
                for h_ in range(TPC // CW):
                    qkv_last[jc] = mm(pb[:, h_ * CW:(h_ + 1) * CW],
                                      gbuf[:, slot * NDC * 128 + dc * 128:slot * NDC * 128 + (dc + 1) * 128],
                                      xn_sb[:, dc * TPC + h_ * CW:dc * TPC + (h_ + 1) * CW],
                                      start=(dc == 0), stop=(dc == NDC - 1),
                                      inc=eT if (dc == NDC - 1 and h_ == TPC // CW - 1) else None)
            nxt = jc + 2
            if nxt < NJC:
                wait("p", eT, qkv_last[jc])
                tqld[slot] = wblk(wsem,
                                  gbuf[:, slot * NDC * 128:(slot + 1) * NDC * 128],
                                  qkT, nxt * 128, ESL // 128)
            h = jc // 3
            s_ = jc % 3
            sslot = jc % 2
            sls = [S_["dST0"], S_["dST1"]]
            dst = stage[:, sslot * TPC:(sslot + 1) * TPC]
            wait("s", eT, qkv_last[jc])
            wait("s", sls[sslot], sls[sslot].n)
            t_evq[jc] = op("s", eS, "activation", out=dst, in_=pb,
                           func=AF.Identity, bias=bqkc[:, jc:jc + 1],
                           scale=float(128.0 ** 0.5) if s_ == 0 else 1.0)
            if s_ == 0:
                wait("g", eS, t_evq[jc])
                t_qst = dma("g", sls[sslot], q_loc[h], dst)
            elif s_ == 1:
                wait("g", eS, t_evq[jc])
                kbase, _ = kv_kbase(h)
                t_kvst = dma("g", sls[sslot],
                             kvf[kbase:kbase + 128 * TPC].rearrange("(p cc) -> p cc", p=128),
                             dst)
            elif PHASE == "vfm":
                wait("g", eS, t_evq[jc])
                vbase, _ = kv_vbase(h)
                t_kvst = dma("g", sls[sslot],
                             kvf[vbase:vbase + TPC * 128].rearrange("(p cc) -> p cc", p=128),
                             dst)
            else:
                ntile = TPC // 128
                vps = PSB[:, 0:TPC]
                t_tr = None
                for tc in range(ntile):
                    wait("t", eS, t_evq[jc])
                    if tc == 0 and ST.get("vtr_ev") is not None:
                        wait("t", eV, ST["vtr_ev"])
                    t_tr = tpose(vps[:, tc * 128:(tc + 1) * 128], dst[:, tc * 128:(tc + 1) * 128],
                                 inc=eT if tc == ntile - 1 else None)
                oslot = 1 - sslot
                wait("v", eT, t_tr)
                wait("v", sls[oslot], sls[oslot].n)
                vdst = stage[:, oslot * TPC:(oslot + 1) * TPC]
                tev = op("v", eV, "tensor_copy", out=vdst, in_=vps)
                ST["vtr_ev"] = tev
                wait("g", eV, tev)
                vbase, _ = kv_vbase(h)
                t_kvst = dma("g", sls[oslot],
                             kvf[vbase:vbase + TPC * 128].rearrange(
                                 "(tc p f) -> p tc f", p=128, f=128),
                             vdst.rearrange("p (tc f) -> p tc f", f=128))
            if jc == 3 * HG0 - 1 and PHASE not in ("qx", "kx", "vx", "vfm"):
                wait("g", S_["dST0"], S_["dST0"].n)
                wait("g", S_["dST1"], S_["dST1"].n)
                t_cckv = S_["ccs"].tok()
                on("g", lambda eng: eng.collective_compute(
                    "AllGather", ALU.bypass, ins=[kv_loc[0:GH0 // 1024, :]], outs=[kv_g0],
                    replica_groups=[list(range(NC))]).then_inc(S_["ccs"].h, 1))

        wait("g", S_["dST0"], S_["dST0"].n)
        wait("g", S_["dST1"], S_["dST1"].n)
        if PHASE in ("qx", "kx", "vx", "vfm"):
            t_out = None
            for h_ in range(H):
                if PHASE == "qx":
                    t_out = dma("g", S_["dOUT"], y[h_ * 128:(h_ + 1) * 128, :], q_loc[h_])
                elif PHASE == "kx":
                    kb2, _ = kv_kbase(h_)
                    t_out = dma("g", S_["dOUT"], y[h_ * 128:(h_ + 1) * 128, :],
                                kvf[kb2:kb2 + 128 * TPC].rearrange(
                                    "(p cc) -> p cc", p=128))
                elif PHASE == "vfm":
                    vb2, _ = kv_vbase(h_)
                    t_out = dma("g", S_["dOUT"], y[h_ * 128:(h_ + 1) * 128, :],
                                kvf[vb2:vb2 + TPC * 128].rearrange(
                                    "(p cc) -> p cc", p=128))
                else:
                    vb2, _ = kv_vbase(h_)
                    t_out = dma("g", S_["dOUT"], y[h_ * 128:(h_ + 1) * 128, :],
                                kvf[vb2:vb2 + TPC * 128].rearrange(
                                    "(l f) -> f l", f=128))
            wait("g", S_["dOUT"], t_out)
            return _finish(nc, en, Q)
        if PHASE in ("qx", "kx", "vx", "vfm"):
            t_cckv = S_["ccs"].tok()
            on("g", lambda eng: eng.collective_compute(
                "AllGather", ALU.bypass, ins=[kv_loc[0:GH0 // 1024, :]], outs=[kv_g0],
                replica_groups=[list(range(NC))]).then_inc(S_["ccs"].h, 1))
        t_cckv2 = S_["ccs"].tok()
        on("g", lambda eng: eng.collective_compute(
            "AllGather", ALU.bypass, ins=[kv_loc[GH0 // 1024:, :]], outs=[kv_g1],
            replica_groups=[list(range(NC))]).then_inc(S_["ccs"].h, 1))

        kslot = [xn_sb[:, 0:S], xn_sb[:, S:2 * S], xn_sb[:, 2 * S:3 * S]]
        vslot = [xn_sb[:, 3 * S:4 * S], cbig[:, 0:S]]
        at0 = S
        atslot = [cbig[:, at0 + i * QL:at0 + (i + 1) * QL] for i in range(3)]
        mrow = [cbig[0:1, at0 + (3 + i) * QL:at0 + (4 + i) * QL] for i in range(2)]
        rsrow = [cbig[0:1, at0 + (5 + i) * QL:at0 + (6 + i) * QL] for i in range(2)]
        qslot = [cbig[:, at0 + (7 + i) * QL:at0 + (8 + i) * QL] for i in range(3)]
        mcol = cbig[:, at0 + 10 * QL:at0 + 10 * QL + 8]
        rsb = [cbig[:, at0 + 10 * QL + 16 + i * QL:at0 + 10 * QL + 16 + (i + 1) * QL]
               for i in range(2)]
        accsb = cbig[:, at0 + 10 * QL + 16 + 2 * QL:at0 + 10 * QL + 16 + 3 * QL]
        KHs = [HG0 * 128 * TPC, HG1 * 128 * TPC]
        kKs = [kvgs[g][:, 0:KHs[g]].rearrange("r (d l) -> d r l", l=TPC) for g in range(2)]
        kVs = [kvgs[g][:, KHs[g]:].rearrange("r (h l f) -> h r l f", l=TPC, f=128) for g in range(2)]

        useq = [(b_, h_) for h_ in range(HG0) for b_ in range(B)] + \
               [(b_, h_) for h_ in range(HG0, H) for b_ in range(B)]
        NB1 = S // 512
        NP1 = NQT * NB1
        ksem = ["kvA", "kvB", "kvC"]
        qsem = ["qA", "qB", "qC"]

        def ccwait(ui):
            g = kv_group(useq[ui][1])
            wait("p", S_["ccs"], t_cckv2 if g == 1 else t_cckv)

        def loadK(ui):
            b_, h_ = useq[ui]
            g = kv_group(h_)
            hl = h_ - (0 if g == 0 else HG0)
            ccwait(ui)
            return dma("p", S_[ksem[ui % 3]],
                       kslot[ui % 3].rearrange("p (r cc) -> p r cc", r=NC),
                       kKs[g][hl * 128:(hl + 1) * 128, :, b_ * QL:(b_ + 1) * QL])

        def loadQ(ui):
            b_, h_ = useq[ui]
            ccwait(ui)
            return dma("p", S_[qsem[ui % 3]],
                       qslot[ui % 3], q_loc[h_][:, b_ * QL:(b_ + 1) * QL])

        def loadV(ui):
            b_, h_ = useq[ui]
            g = kv_group(h_)
            hl = h_ - (0 if g == 0 else HG0)
            ccwait(ui)
            vd4 = vslot[ui % 2].rearrange("p (r sb f) -> p r sb f", r=NC, f=128)
            vs4 = kVs[g][hl, :, b_ * QL:(b_ + 1) * QL, :].rearrange(
                "r (sb p) f -> p r sb f", p=128)
            t = None
            for sb in range(QL // 128):
                t = dma("p", S_["vvA" if ui % 2 == 0 else "vvB"],
                        vd4[:, :, sb, :], vs4[:, :, sb, :])
            return t

        def uu_of(w):
            return useq[w][0] * H + useq[w][1]

        t_o_ev = {}
        kq_tok, vv_tok = {}, {}
        wait("g", eT, eT.n)
        wait("g", S_["dST0"], S_["dST0"].n)
        wait("g", S_["dST1"], S_["dST1"].n)
        tmz = op("g", eG, "memset", ap=cbig[:, at0 + 3 * QL:at0 + 5 * QL],
                 constant=0.0)
        tmz2 = op("g", eG, "memset", ap=stage[:, 0:128], constant=0.0)
        ST["mz"] = tmz
        wait("s", eG, tmz2)
        wait("s", eV, eV.n)
        op("s", eS, "copy", out=stage[0:1, 0:128], in_=ones_bf[:])
        mrowpad = [cbig[:, at0 + (3 + i) * QL:at0 + (4 + i) * QL] for i in range(2)]

        def p1_item(u_, j):
            qt, kb = j // NB1, j % NB1
            bk = j % 2
            pba = PS[:, bk * 512:bk * 512 + 512]
            prev = ST.get(("p1bank", bk))
            if prev is not None:
                wait("t", eV, prev)
            t_mm = mm(pba, qslot[u_ % 3][:, qt * QTW:(qt + 1) * QTW],
                      kslot[u_ % 3][:, kb * 512:(kb + 1) * 512],
                      start=True, stop=True, inc=eT)
            wait("v", eT, t_mm)
            t_mx = op("v", eV, "tensor_reduce", out=sm[:, j:j + 1], in_=pba,
                      axis=AX.X, op=ALU.max, negate=True)
            ST[("p1bank", bk)] = t_mx
            if kb == NB1 - 1:
                wait("v", eV, t_mx)
                if NB1 > 1:
                    t_cb = op("v", eV, "tensor_reduce", out=sm[:, 32 + qt:33 + qt],
                              in_=sm[:, qt * NB1:(qt + 1) * NB1], axis=AX.X, op=ALU.min)
                else:
                    t_cb = op("v", eV, "tensor_copy", out=sm[:, 32 + qt:33 + qt],
                              in_=sm[:, j:j + 1])
                prev_t = ST.get(("mtr", qt))
                if prev_t is not None:
                    wait("v", eT, prev_t)
                wait("v", eV, t_cb)
                ST[("mcol", qt)] = op("v", eV, "tensor_copy", _lp=True,
                                      out=mcol[:, qt:qt + 1],
                                      in_=sm[:, 32 + qt:33 + qt])

        def p1_tail(u_):
            t_tr = None
            for qt in range(NQT):
                wait("t", eV, ST[("mcol", qt)])
                if qt == 0:
                    prev_m = ST.get(("mrow", u_ - 1))
                    if prev_m is not None:
                        wait("t", eS, prev_m)
                t_tr = tpose(PSB[0:1, qt * QTW:(qt + 1) * QTW],
                             mcol[:, qt:qt + 1], inc=eT)
                ST[("mtr", qt)] = t_tr
            prev_b = ST.get(("p2mm", u_ - 2))
            if prev_b is not None:
                wait("s", eT, prev_b)
            wait("s", eT, t_tr)
            ST[("mrow", u_)] = op("s", eS, "copy", _lp=True, out=mrow[u_ % 2],
                                  in_=PSB[0:1, 0:QL])

        def p2_consume(w, kc):
            sl = kc % 3
            obw = 4 + (w % 2)
            ob_ap = PS[:, obw * 512:obw * 512 + QL]
            t_exp = ST[("pexp", w)][kc]
            wait("t", eS, t_exp)
            if kc == 0:
                wait("t", S_["vvA" if w % 2 == 0 else "vvB"], vv_tok[w])
                prev_o = ST.get(("oev", w - 1))
                if prev_o is not None:
                    wait("t", eV, prev_o)
            t_av = mm(ob_ap, vslot[w % 2][:, kc * 128:(kc + 1) * 128], atslot[sl],
                      start=(kc == 0), stop=(kc == NKC - 1), inc=eT)
            wait("v", eS, t_exp)
            if kc == 0:
                prev_pc = ST.get(("rscp", w - 1))
                if prev_pc is not None:
                    wait("v", eS, prev_pc)
                t_ac = op("v", eV, "tensor_copy", out=PSC[:, 0:QL], in_=atslot[sl])
            else:
                wait("v", eV, ST[("acc", w)])
                t_ac = op("v", eV, "tensor_add", out=PSC[:, 0:QL],
                          in0=PSC[:, 0:QL], in1=atslot[sl])
            ST[("acc", w)] = t_ac
            ST[("atrd", sl)] = (t_av, t_ac)
            if kc == NKC - 1:
                ST[("av", w)] = t_av

        def p2_item(w, kc):
            pb = 2 + (kc % 2)
            pba = PS[:, pb * 512:pb * 512 + QL]
            prev_exp = ST.get(("p2bank", pb))
            if prev_exp is not None:
                wait("t", eS, prev_exp)
            if kc == 0:
                wait("t", eS, ST[("mrow", w)])
                if w == 0:
                    wait("t", eG, ST["mz"])
                ST[("pexp", w)] = {}
            mm(pba, stage[:, 0:128], mrowpad[w % 2], start=True, stop=False)
            t_sc = mm(pba, kslot[w % 3][:, kc * 128:(kc + 1) * 128], qslot[w % 3],
                      start=False, stop=True, inc=eT)
            ST[("p2mm", w)] = t_sc
            sl = kc % 3
            wait("s", eT, t_sc)
            prev_rd = ST.get(("atrd", sl))
            if prev_rd is not None:
                wait("s", eT, prev_rd[0])
                wait("s", eV, prev_rd[1])
            t_exp = op("s", eS, "activation", _lp=True, out=atslot[sl], in_=pba,
                       func=AF.Exp, bias=0.0, scale=1.0)
            ST[("pexp", w)][kc] = t_exp
            ST[("p2bank", pb)] = t_exp
            if kc >= 1:
                p2_consume(w, kc - 1)

        def p2_tail(w):
            ww = uu_of(w)
            p2_consume(w, NKC - 1)
            wait("s", eV, ST[("acc", w)])
            prev_on = ST.get(("ones", w - 1))
            if prev_on is not None:
                wait("s", eT, prev_on)
            t_acp = op("s", eS, "copy", _lp=True, out=accsb, in_=PSC[:, 0:QL])
            wait("t", eS, t_acp)
            t_ones = mm(PSC[0:1, 0:QL], ones_col[:], accsb, start=True, stop=True,
                        inc=eT)
            ST[("ones", w)] = t_ones
            wait("v", eT, t_ones)
            prev_rs = ST.get(("rsmm", w - 2))
            if prev_rs is not None:
                wait("v", eT, prev_rs)
            t_rc = op("v", eV, "reciprocal", _lp=True, out=rsrow[w % 2],
                      in_=PSC[0:1, 0:QL])
            wait("t", eV, t_rc)
            t_rsmm = mm(PSC[:, 0:QL], ones_bf[:], rsrow[w % 2], start=True, stop=True,
                        inc=eT)
            ST[("rsmm", w)] = t_rsmm
            wait("s", eT, t_rsmm)
            prev_rb = ST.get(("rsb", w - 2))
            if prev_rb is not None:
                wait("s", eV, prev_rb)
            t_rsc = op("s", eS, "copy", _lp=True, out=rsb[w % 2], in_=PSC[:, 0:QL])
            ST[("rscp", w)] = t_rsc
            wait("v", eS, t_rsc)
            obw = 4 + (w % 2)
            t_o_ev[ww] = op("v", eV, "tensor_mul", _lp=True,
                            out=fbig[:, ww * QL:(ww + 1) * QL],
                            in0=PS[:, obw * 512:obw * 512 + QL], in1=rsb[w % 2])
            ST[("oev", w)] = t_o_ev[ww]
            ST[("rsb", w)] = t_o_ev[ww]

        NS = max(NP1, NKC)
        for u in range(NU + 1):
            if u == 0:
                wait("p", eT, eT.n)
                for pre in range(min(3, NU)):
                    kq_tok[pre] = (loadK(pre), loadQ(pre))
                for pre in range(min(2, NU)):
                    vv_tok[pre] = loadV(pre)
            if u < NU:
                ktok, qtok = kq_tok[u]
                wait("t", S_[ksem[u % 3]], ktok)
                wait("t", S_[qsem[u % 3]], qtok)
                if u == 0:
                    wait("t", eV, eV.n)
                    wait("t", eS, eS.n)
            for j in range(NS):
                if u < NU:
                    for i1 in range(j * NP1 // NS, (j + 1) * NP1 // NS):
                        p1_item(u, i1)
                if u >= 1:
                    for i2 in range(j * NKC // NS, (j + 1) * NKC // NS):
                        p2_item(u - 1, i2)
            if u < NU:
                p1_tail(u)
            if u >= 1:
                p2_tail(u - 1)
                if u + 1 < NU and u + 1 >= 2:
                    wait("p", eT, ST[("av", u - 1)])
                    vv_tok[u + 1] = loadV(u + 1)
                if u + 2 < NU:
                    wait("p", eT, ST[("p2mm", u - 1)])
                    kq_tok[u + 2] = (loadK(u + 2), loadQ(u + 2))

        if PHASE == "ox":
            t_out = None
            for u in range(NU):
                b_, h_ = u // H, u % H
                wait("g", eV, t_o_ev[u])
                t_out = dma("g", S_["dOUT"],
                            y[h_ * 128:(h_ + 1) * 128, b_ * QL:(b_ + 1) * QL],
                            fbig[:, u * QL:(u + 1) * QL])
            wait("g", S_["dOUT"], t_out)
            return _finish(nc, en, Q)
        tpld = {0: None, 1: None}
        for pre in range(min(2, NDC)):
            psl = pre % 2
            wait("p", eT, qkv_last[NJC - 1])
            tpld[psl] = wblk(S_["wsA" if psl == 0 else "wsB"],
                             gbuf[:, psl * NDC * 128:(psl + 1) * NDC * 128],
                             prT, pre * 128, ESL // 128)
        pr_last, t_evp, t_res2 = {}, {}, {}
        for dc in range(NDC):
            slot = dc % 2
            wsem = S_["wsA" if slot == 0 else "wsB"]
            wait("t", wsem, tpld[slot])
            for b_ in range(B):
                pbi = (dc * B + b_) % 6
                pb = PS[:, pbi * 512:pbi * 512 + QL]
                if (dc - 3, b_) in t_evp:
                    wait("t", eS, t_evp[(dc - 3, b_)])
                elif dc == 0 and b_ == 0:
                    wait("t", eV, eV.n)
                    wait("t", eS, eS.n)
                for ec in range(NDC):
                    uu = b_ * H + ec
                    if dc == 0:
                        wait("t", eV, t_o_ev[uu])
                    pr_last[(dc, b_)] = mm(
                        pb, gbuf[:, slot * NDC * 128 + ec * 128:slot * NDC * 128 + (ec + 1) * 128],
                        fbig[:, uu * QL:(uu + 1) * QL],
                        start=(ec == 0), stop=(ec == NDC - 1), inc=eT if ec == NDC - 1 else None)
                wait("s", eT, pr_last[(dc, b_)])
                sl = pbi % 2
                zst = stage[:, sl * TPC:sl * TPC + QL]
                prevz = ST.get(("zst", sl))
                if prevz is not None:
                    wait("s", eV, prevz)
                else:
                    wait("s", S_["dST0"], S_["dST0"].n)
                    wait("s", S_["dST1"], S_["dST1"].n)
                t_evp[(dc, b_)] = op("s", eS, "activation", out=zst, in_=pb,
                                     func=AF.Identity, bias=bprc[:, dc:dc + 1], scale=1.0)
                wait("v", eS, t_evp[(dc, b_)])
                t_res2[dc] = op("v", eV, "tensor_add",
                                out=x_sb[:, dc * TPC + b_ * QL:dc * TPC + (b_ + 1) * QL],
                                in0=x_sb[:, dc * TPC + b_ * QL:dc * TPC + (b_ + 1) * QL],
                                in1=zst)
                ST[("zst", sl)] = t_res2[dc]
            nxt = dc + 2
            if nxt < NDC:
                wait("p", eT, pr_last[(dc, B - 1)])
                tpld[slot] = wblk(wsem,
                                  gbuf[:, slot * NDC * 128:(slot + 1) * NDC * 128],
                                  prT, nxt * 128, ESL // 128)

        if PHASE == "attn":
            t_out = None
            for dc in range(NDC):
                wait("g", eV, t_res2[dc])
                t_out = dma("g", S_["dOUT"], y[dc * 128:(dc + 1) * 128, :],
                            x_sb[:, dc * TPC:(dc + 1) * TPC])
            wait("g", S_["dOUT"], t_out)
            return _finish(nc, en, Q)
        ST[("vfence", "ln3")] = eV.n
        ST[("sfence", "ln3")] = eS.n
        ST[("lnfence", "ln3")] = eT.n
        tv_xn3 = emit_ln("ln3", t_res2[NDC - 1])
        t_res3 = emit_ffn("ffn2", tv_xn3, first=False)

        t_out = None
        for dc in range(NDC):
            wait("g", eV, t_res3[dc])
            t_out = dma("g", S_["dOUT"], y[dc * 128:(dc + 1) * 128, :],
                        x_sb[:, dc * TPC:(dc + 1) * TPC])
        wait("g", S_["dOUT"], t_out)

        return _finish(nc, en, Q)


def _finish(nc, en, Q):
    block = en(nc.Block())

    @block.tensor
    def _(eng):
        for f in Q["t"]:
            f(eng)

    @block.vector
    def _(eng):
        for f in Q["v"]:
            f(eng)

    @block.scalar
    def _(eng):
        for f in Q["s"]:
            f(eng)

    @block.sync
    def _(eng):
        for f in Q["p"]:
            f(eng)

    @block.gpsimd
    def _(eng):
        for f in Q["g"]:
            f(eng)

    return nc



def make_in_maps(cfg, I):
    c = derive(cfg)
    B, S, E = c["B"], c["S"], c["E"]
    QL, ESL, FSL = c["QL"], c["ESL"], c["FSL"]
    NJC, NDC, NHC = c["NJC"], c["NDC"], c["NHC"]
    x = np.asarray(I["x"], dtype=np.float32)
    cc = np.ascontiguousarray
    maps = []
    for r in range(NC):
        maps.append(dict(
            xs=cc(x[:, r * QL:(r + 1) * QL, :].reshape(B * QL, E).T),
            wupq_s=cc(I["w_up_q"][:, r * ESL:(r + 1) * ESL]),
            wups_s=cc(I["w_up_scale"][:, r * (ESL // 64):(r + 1) * (ESL // 64)]),
            wdnq_s=cc(I["w_down_q"][:, r * FSL:(r + 1) * FSL]),
            wdns_s=cc(I["w_down_scale"][:, r * (FSL // 64):(r + 1) * (FSL // 64)]),
            wqkv_s=cc(I["Wqkv"][:, r * ESL:(r + 1) * ESL]),
            wproj_s=cc(I["Wproj"][:, r * ESL:(r + 1) * ESL]),
            bqkv=cc(I["bqkv"].reshape(NJC, 128)),
            bproj=cc(I["bproj"].reshape(NDC, 128)),
            bup=cc(I["b_up"].reshape(NHC, 128)),
            bdn=cc(I["b_down"].reshape(NDC, 128)),
            aup=cc(I["w_up_lora_a"]),
            bupl=cc(I["w_up_lora_b"]),
            adn=cc(I["w_down_lora_a"]),
            bdnl=cc(I["w_down_lora_b"]),
        ))
    return maps


def assemble_out(cfg, results):
    c = derive(cfg)
    B, S, E, QL = c["B"], c["S"], c["E"], c["QL"]
    out = np.empty((B, S, E), dtype=np.float32)
    for r in range(NC):
        out[:, r * QL:(r + 1) * QL, :] = np.asarray(results[r]["y"]).T.reshape(B, QL, E)
    return out


def kernel(L0_attn_query, L0_attn_key, L0_attn_value, L0_attn_Wqkv, L0_attn_bqkv,
           L0_attn_Wproj, L0_attn_bproj, L0_ffn_w_up_q, L0_ffn_w_up_scale, L0_ffn_b_up,
           L0_ffn_w_up_lora_a, L0_ffn_w_up_lora_b, L0_ffn_w_down_q, L0_ffn_w_down_scale,
           L0_ffn_b_down, L0_ffn_w_down_lora_a, L0_ffn_w_down_lora_b):
    cfg = full_cfg()
    I = dict(
        x=np.asarray(L0_attn_query, dtype=np.float32),
        Wqkv=np.asarray(L0_attn_Wqkv, dtype=np.float32),
        bqkv=np.asarray(L0_attn_bqkv, dtype=np.float32),
        Wproj=np.asarray(L0_attn_Wproj, dtype=np.float32),
        bproj=np.asarray(L0_attn_bproj, dtype=np.float32),
        w_up_q=np.asarray(L0_ffn_w_up_q, dtype=np.int32),
        w_up_scale=np.asarray(L0_ffn_w_up_scale, dtype=np.float32),
        b_up=np.asarray(L0_ffn_b_up, dtype=np.float32),
        w_up_lora_a=np.asarray(L0_ffn_w_up_lora_a, dtype=np.float32),
        w_up_lora_b=np.asarray(L0_ffn_w_up_lora_b, dtype=np.float32),
        w_down_q=np.asarray(L0_ffn_w_down_q, dtype=np.int32),
        w_down_scale=np.asarray(L0_ffn_w_down_scale, dtype=np.float32),
        b_down=np.asarray(L0_ffn_b_down, dtype=np.float32),
        w_down_lora_a=np.asarray(L0_ffn_w_down_lora_a, dtype=np.float32),
        w_down_lora_b=np.asarray(L0_ffn_w_down_lora_b, dtype=np.float32),
    )
    nc = build(cfg)
    maps = make_in_maps(cfg, I)
    res = run_bass_kernel_spmd(nc, maps, core_ids=list(range(NC)))
    return assemble_out(cfg, res.results)
